# version 01 (final)
# baseline (speedup 1.0000x reference)
import numpy as np
import ml_dtypes
import concourse.bass as bass
import concourse.mybir as mybir
from concourse.bass_utils import run_bass_kernel_spmd
from concourse.bass_core_rust import IndirectOffsetOnAxis

F32 = mybir.dt.float32
BF16 = mybir.dt.bfloat16
U32 = mybir.dt.uint32
AF = mybir.ActivationFunctionType
ALU = mybir.AluOpType
AX = mybir.AxisListType

D = 1024
H = 16
T4 = 4096
T2 = 2048
E = 8
FF = 4096
CAP = 640
SLOTS = CAP * E
NEG = -30000.0
EPS = 1e-5
BIG = 1.0e6

SU2_N = 99
SU2_PART = 4
NOLN = False
A1_MODE = "full"
STOP_AFTER = None
DBG_TAPS = False


class _Stub:
    def __getattr__(self, name):
        def f(*a, **kw):
            return _SI()
        return f


class _SI:
    def then_inc(self, *a, **kw):
        return self

    def __getattr__(self, name):
        def f(*a, **kw):
            return self
        return f


class K:
    ENGS = (("TE", "tensor"), ("VE", "vector"), ("AC", "scalar"),
            ("GE", "gpsimd"))
    DMASEMS = ("LD", "GD", "LX", "LW", "LK", "LM2", "GSQ", "GSK", "GSV",
               "GC0", "GC1", "GO", "GI")

    def __init__(self, nc):
        self.nc = nc
        self.sem = {}
        self.cnt = {}
        for name in ("LD", "GD", "TE", "VE", "AC", "GE", "LX", "LW", "LK",
                     "LM2", "GSQ", "GSK", "GSV", "GC0", "GC1", "GO", "GI"):
            self.sem[name] = nc.alloc_semaphore(name)
            self.cnt[name] = 0
        self.dram = {}
        self.labels = {}
        self.emitting = False
        self.trace = {n: [] for n in
                      ("sync", "tensor", "vector", "scalar", "gpsimd")}
        self.cur_eng = None

    def inc(self, ins, s):
        self.cnt[s] += 1
        if self.emitting:
            ins.then_inc(self.sem[s], 1)
            self.trace[self.cur_eng].append(("inc", s, 1))

    def dinc(self, ins, s):
        self.cnt[s] += 16
        if self.emitting:
            ins.then_inc(self.sem[s], 16)
            self.trace[self.cur_eng].append(("inc", s, 16))

    def mark(self, label, s):
        if not self.emitting:
            self.labels[(label, s)] = self.cnt[s]

    def need(self, eng, label, s):
        if self.emitting:
            eng.wait_ge(self.sem[s], self.labels[(label, s)])
            self.trace[self.cur_eng].append(
                ("wait", s, self.labels[(label, s)], label))

    def sec(self, engname, body):
        if not self.emitting:
            body(_Stub())
        else:
            self.cur_eng = engname
            getattr(self.blk, engname)(body)
            self.cur_eng = None

    def check_deadlock(self):
        pos = {e: 0 for e in self.trace}
        val = {s: 0 for s in self.sem}
        progress = True
        while progress:
            progress = False
            for e, tr in self.trace.items():
                while pos[e] < len(tr):
                    op = tr[pos[e]]
                    if op[0] == "wait":
                        if val[op[1]] < op[2]:
                            break
                    else:
                        val[op[1]] += op[2]
                    pos[e] += 1
                    progress = True
        stuck = {e: (pos[e], len(self.trace[e])) for e in self.trace
                 if pos[e] < len(self.trace[e])}
        if stuck:
            msg = []
            for e in stuck:
                op = self.trace[e][pos[e]]
                msg.append(f"{e}@{pos[e]}/{len(self.trace[e])} waits "
                           f"{op[1]}>={op[2]} ({op[3]}) have {val[op[1]]}")
            raise RuntimeError("DEADLOCK: " + "; ".join(msg))
        return True

    def run2(self, fn):
        snap = dict(self.cnt)
        self.emitting = False
        fn()
        self.cnt = snap
        self.emitting = True
        fn()
        self.emitting = False

    def barrier(self, tag):
        k = self
        for s, en in self.ENGS:
            def mkinc(s=s):
                def b(eng):
                    k.inc(eng.drain(), s)
                return b
            self.sec(en, mkinc())
            self.mark(f"bar{tag}{s}", s)
        for ds in self.DMASEMS:
            self.mark(f"bar{tag}{ds}", ds)
        for s, en in self.ENGS:
            def mkwait(s=s):
                def b(eng):
                    for s2, _ in k.ENGS:
                        if s2 != s:
                            k.need(eng, f"bar{tag}{s2}", s2)
                    for ds in k.DMASEMS:
                        k.need(eng, f"bar{tag}{ds}", ds)
                return b
            self.sec(en, mkwait())

        def bsync(sync):
            for s2, _ in k.ENGS:
                k.need(sync, f"bar{tag}{s2}", s2)
            for ds in k.DMASEMS:
                k.need(sync, f"bar{tag}{ds}", ds)
        self.sec("sync", bsync)

    def d_in(self, name, shape, dtype=F32):
        self.dram[name] = self.nc.dram_tensor(name, shape, dtype,
                                              kind="ExternalInput").ap()

    def d_out(self, name, shape, dtype=F32):
        self.dram[name] = self.nc.dram_tensor(name, shape, dtype,
                                              kind="ExternalOutput").ap()

    def d_tmp(self, name, shape, dtype=F32):
        kind = "ExternalOutput" if DBG_TAPS else "Internal"
        self.dram[name] = self.nc.dram_tensor(name, shape, dtype,
                                              kind=kind).ap()


def build(nc):
    nc.detect_race_conditions = False
    k = K(nc)
    d = k.d_in
    d("x", [T4, D])
    for n in ("wqT", "wkT", "wvT", "woT"):
        d(n, [D, D])
    d("wqkv", [D, 3 * D]); d("bqkv", [3 * D])
    d("fwo", [D, D]); d("bo", [D])
    d("ln1g", [D]); d("ln1b", [D]); d("ln2g", [D]); d("ln2b", [D])
    d("gatew", [D, E])
    d("w1", [E * D, FF]); d("b1", [E, FF])
    d("w2", [E * FF, D]); d("b2", [E, D])
    d("rw1", [D, FF]); d("rb1", [FF])
    d("rw2", [FF, D]); d("rb2", [D])
    d("coefw", [D, 8]); d("coefb", [2])
    d("identf", [128, 128]); d("identb", [128, 128], BF16)
    d("masklo4", [128, 512]); d("maskhi4", [128, 512])
    d("hsumb", [8 * 128, 16], BF16)
    d("haloAin", [128, 1]); d("haloBin", [128, 1])
    d("trilb", [128, 128], BF16); d("onesb128", [128, 128], BF16)
    d("i16f", [16, 16]); d("i16b", [16, 16], BF16)
    d("sel0b", [128, 128], BF16); d("sel64b", [128, 64], BF16)
    k.d_out("out", [T2, D])
    t = k.d_tmp
    t("x1buf", [T4, D])
    t("xnbbuf", [T2, D])
    t("xlfmH", [D, T4], BF16); t("xlfmL", [D, T4], BF16)
    t("kaH", [D, T4], BF16); t("kaL", [D, T4], BF16)
    t("qaH", [D, T4], BF16); t("qaL", [D, T4], BF16)
    t("kbH", [D, T4], BF16); t("kbL", [D, T4], BF16)
    t("vaH", [H, T4, 65], BF16); t("vaL", [H, T4, 65], BF16)
    t("vbH", [H, T4, 65], BF16); t("vbL", [H, T4, 65], BF16)
    t("ctxaH", [D, T4], BF16); t("ctxaL", [D, T4], BF16)
    t("ctxbH", [D, T2], BF16); t("ctxbL", [D, T2], BF16)
    t("xibuf", [T2, D])
    t("rmbuf", [T2, D])
    t("xn2buf", [T2, D])
    t("xn2fmH", [D, T2], BF16); t("xn2fmL", [D, T2], BF16)
    t("xeH", [SLOTS, D], BF16); t("xeL", [SLOTS, D], BF16)
    t("ybuf", [SLOTS, D])
    dr = k.dram

    PS = [nc.alloc_psum_tensor(f"ps{i}", [128, 512], F32) for i in range(7)]
    PSB = nc.alloc_psum_tensor("psbx", [128, 1024], BF16)

    A = nc.alloc_sbuf_tensor
    idf = A("idf", [128, 128], F32)
    idb = A("idb", [128, 128], BF16)
    hsum = A("hsum_s", [128, 8 * 16], BF16)
    haloA = A("haloA_s", [128, 1], F32)
    haloB = A("haloB_s", [128, 1], F32)
    trilb = A("trilb_s", [128, 128], BF16)
    onesb = A("onesb_s", [128, 128], BF16)
    i16f = A("i16f_s", [16, 16], F32)
    i16b = A("i16b_s", [16, 16], BF16)
    sel0b = A("sel0b_s", [128, 128], BF16)
    sel64b = A("sel64b_s", [128, 64], BF16)
    onesr = A("onesr", [128, 128], F32)
    bqkv = A("bqkv_s", [128, 24], F32)
    b1sb = A("b1sb", [128, E * 32], F32)
    rb1sb = A("rb1sb", [128, 32], F32)
    gwH = A("gwH", [128, 8 * E], BF16); gwL = A("gwL", [128, 8 * E], BF16)
    cwH = A("cwH", [128, 64], BF16); cwL = A("cwL", [128, 64], BF16)
    cbH = A("cbH", [128, 8], BF16); cbL = A("cbL", [128, 8], BF16)
    gcstg = A("gcstg", [128, 128], F32)
    rowb = A("rowb", [128, 1032], F32)
    negbA = A("negbA", [128, H], F32)
    negbB = A("negbB", [128, H], F32)
    n2m = A("n2m", [16, 4], F32)
    brepb = A("brepb", [16, 128], BF16)
    zeros16 = A("zeros16", [16, 128], F32)
    erow = A("erow", [128, E], F32)
    erowmB = A("erowmB", [128, E], F32)
    idxf = A("idxf", [128, 16], F32)
    gvt = A("gvt", [128, 16], F32)
    c0t = A("c0t", [128, 16], F32)
    c1t = A("c1t", [128, 16], F32)
    dest32 = A("dest32", [128, 16], U32)
    oht = A("oht", [128, 16 * E], BF16)
    stats = A("stats", [128, 32], F32)
    cst = A("cst", [128, 16], F32)
    sq32 = A("sq32", [128, 1024], F32)
    sqb = A("sqb", [128, 512], BF16)
    rrow = A("rrow", [128, 512], F32)
    rrowH = A("rrowH", [128, 512], BF16)
    rrowL = A("rrowL", [128, 512], BF16)
    bcsb = A("bcsb", [128, 512], F32)
    evt = A("evt", [128, 1024], F32)


    def ln_stats(tag, src_fn, ntile, width=1024, prev=None):
        def ac(scalar):
            for i in range(ntile):
                if i >= 1:
                    k.need(scalar, f"{tag}rs{i - 1}", "VE")
                elif prev is not None:
                    k.need(scalar, prev[0], prev[1])
                scalar.square(sq32[:, 0:width], src_fn(i))
                k.inc(scalar.drain(), "AC")
                k.mark(f"{tag}sq{i}", "AC")
        k.sec("scalar", ac)

        def ve(vector):
            for i in range(ntile):
                vector.reduce_sum(stats[:, i:i + 1], src_fn(i), axis=AX.X)
                k.need(vector, f"{tag}sq{i}", "AC")
                vector.reduce_sum(stats[:, 4 + i:5 + i], sq32[:, 0:width],
                                  axis=AX.X)
                k.inc(vector.drain(), "VE")
                k.mark(f"{tag}rs{i}", "VE")
            n = ntile
            assert width == 1024
            vector.tensor_scalar(stats[:, 8:8 + n], stats[:, 0:n],
                                 cst[:, 2:3], None, ALU.mult)
            vector.tensor_scalar(stats[:, 16:16 + n], stats[:, 4:4 + n],
                                 cst[:, 2:3], None, ALU.mult)
            vector.drain()
            vector.tensor_mul(stats[:, 20:20 + n], stats[:, 8:8 + n],
                              stats[:, 8:8 + n])
            vector.drain()
            vector.tensor_sub(stats[:, 24:24 + n], stats[:, 16:16 + n],
                              stats[:, 20:20 + n])
            vector.tensor_scalar(stats[:, 16:16 + n], stats[:, 24:24 + n],
                                 cst[:, 1:2], None, ALU.add)
            k.inc(vector.drain(), "VE")
            k.mark(f"{tag}var", "VE")
        k.sec("vector", ve)

        def ac2(scalar):
            k.need(scalar, f"{tag}var", "VE")
            scalar.activation(out=stats[:, 24:24 + ntile],
                              in_=stats[:, 16:16 + ntile], func=AF.Ln)
            scalar.activation(out=stats[:, 12:12 + ntile],
                              in_=stats[:, 24:24 + ntile], func=AF.Exp,
                              scale=-0.5)
            k.inc(scalar.drain(), "AC")
            k.mark(f"{tag}r", "AC")
        k.sec("scalar", ac2)

        def ve2(vector):
            k.need(vector, f"{tag}r", "AC")
            n = ntile
            vector.tensor_mul(stats[:, 24:24 + n], stats[:, 8:8 + n],
                              stats[:, 12:12 + n])
            vector.tensor_scalar(stats[:, 20:20 + n],
                                 stats[:, 24:24 + n], cst[:, 0:1], None,
                                 ALU.mult)
            k.inc(vector.drain(), "VE")
            k.mark(f"{tag}B", "VE")
        k.sec("vector", ve2)

    def ph_setup():
        def ld(sync):
            for name, dst in (("identf", idf), ("identb", idb),
                              ("haloAin", haloA), ("haloBin", haloB),
                              ("trilb", trilb), ("onesb128", onesb),
                              ("i16f", i16f), ("i16b", i16b),
                              ("sel0b", sel0b), ("sel64b", sel64b)):
                k.dinc(sync.dma_start(out=dst[:, :], in_=dr[name][:, :]), "LD")
            k.dinc(sync.dma_start(
                out=hsum[:, :].rearrange("p (c h) -> p c h", h=16),
                in_=dr["hsumb"].rearrange("(c p) h -> p c h", p=128),
                allow_slow_non_contiguous=True), "LD")
            k.dinc(sync.dma_start(
                out=bqkv[:, :],
                in_=dr["bqkv"].rearrange("(c p) -> p c", p=128),
                allow_slow_non_contiguous=True), "LD")
            k.dinc(sync.dma_start(
                out=b1sb[:, :].rearrange("p (e c) -> p e c", c=32),
                in_=dr["b1"].rearrange("e (c p) -> p e c", p=128),
                allow_slow_non_contiguous=True), "LD")
            k.dinc(sync.dma_start(
                out=rb1sb[:, :],
                in_=dr["rb1"].rearrange("(c p) -> p c", p=128),
                allow_slow_non_contiguous=True), "LD")
            k.dinc(sync.dma_start(
                out=gcstg[:, 0:64].rearrange("p (c e) -> p c e", e=8),
                in_=dr["gatew"].rearrange("(c p) e -> p c e", p=128),
                allow_slow_non_contiguous=True), "LD")
            k.dinc(sync.dma_start(
                out=gcstg[:, 64:128].rearrange("p (c e) -> p c e", e=8),
                in_=dr["coefw"].rearrange("(c p) e -> p c e", p=128),
                allow_slow_non_contiguous=True), "LD")
        k.sec("sync", ld)
        k.mark("su_ld", "LD")

        def ld1b(sync):
            k.need(sync, "su_z", "VE")
            k.dinc(sync.dma_start(out=rowb[0:1, 1024:1026],
                                  in_=dr["coefb"][:]), "LM2")
            k.mark("su_cb", "LM2")
        k.sec("sync", ld1b)

        def ve(vector):
            vector.memset(onesr[:, :], 1.0)
            vector.memset(zeros16[:, :], 0.0)
            vector.memset(n2m[:, :], 0.0)
            vector.memset(evt[:, :], 0.0)
            vector.memset(rowb[:, :], 0.0)
            vector.memset(rspH[:, :], 0.0)
            vector.memset(rspL[:, :], 0.0)
            vector.memset(rsp2[:, :], 0.0)
            vector.memset(rrowH[:, :], 0.0)
            vector.memset(rrowL[:, :], 0.0)
            vector.memset(cbH[:, :], 0.0)
            vector.memset(cbL[:, :], 0.0)
            for col, val in ((0, -1.0), (1, EPS), (2, 1.0 / 1024.0),
                             (3, 1.0 / 64.0), (4, 85.0), (5, -BIG),
                             (6, float(CAP)), (7, BIG), (8, NEG), (9, 1.0),
                             (10, -0.5), (11, 1.5), (12, float(SLOTS - 1))):
                vector.memset(cst[:, col:col + 1], val)
            k.inc(vector.drain(), "VE")
            k.mark("su_z", "VE")
            k.need(vector, "su_ld", "LD")
            g = gcstg[:, 0:64]
            vector.tensor_copy(gwH[:, :], g)
            vector.tensor_tensor(gwL[:, :], g, gwH[:, :], ALU.subtract)
            cw = gcstg[:, 64:128]
            vector.tensor_copy(cwH[:, :], cw)
            vector.tensor_tensor(cwL[:, :], cw, cwH[:, :], ALU.subtract)
            k.need(vector, "su_cb", "LM2")
            cb = rowb[0:1, 1024:1026]
            vector.tensor_copy(cbH[0:1, 0:2], cb)
            vector.tensor_tensor(cbL[0:1, 0:2], cb, cbH[0:1, 0:2],
                                 ALU.subtract)
            k.inc(vector.drain(), "VE")
        k.sec("vector", ve)
        k.mark("su_ve", "VE")

        def ge0(gpsimd):
            gpsimd.iota(erow[:, :], [[1, E]], channel_multiplier=0,
                        allow_small_or_imprecise_dtypes=True)
            k.inc(gpsimd.drain(), "GE")
        k.sec("gpsimd", ge0)
        k.mark("su_ge", "GE")

        def ve1(vector):
            k.need(vector, "su_ge", "GE")
            vector.tensor_scalar(erowmB[:, :], erow[:, :], cst[:, 5:6], None,
                                 ALU.add)
            k.inc(vector.drain(), "VE")
        k.sec("vector", ve1)

        def ge(gpsimd):
            k.need(gpsimd, "su_z", "VE")
            for buf in ("xeH", "xeL"):
                for i in range(SLOTS // 128):
                    k.dinc(gpsimd.dma_start(out=dr[buf][i * 128:(i + 1) * 128, :],
                                            in_=evt[:, :]), "GI")
            k.mark("su_xez", "GI")
        k.sec("gpsimd", ge)
        k.barrier("SU")

    ph_setup2_refs = {}
    rspH = A("rspH", [128, 1024], BF16)
    rspL = A("rspL", [128, 1024], BF16)
    rsp2 = A("rsp2", [128, 1024], BF16)

    def ph_setup2():
        R = ph_setup2_refs
        rows = [("bo", R["boT"]), ("rb2", R["rb2T"]), ("ln1g", R["g1T"]),
                ("ln1b", R["b1T"]), ("ln2g", R["g2T"]), ("ln2b", R["b2T"])]
        rows = rows[:SU2_N]

        def ld2(sync):
            for i, (name, dst) in enumerate(rows):
                if i >= 1:
                    k.need(sync, f"su_rsp{i - 1}", "VE")
                k.dinc(sync.dma_start(out=rowb[0:1, 0:1024], in_=dr[name][:]),
                       "LM2")
                k.mark(f"su_row{i}", "LM2")
            if SU2_N > 6:
                k.need(sync, f"su_rsp{len(rows) - 1}", "VE")
                k.dinc(sync.dma_start(out=rowb[0:1, 0:1024],
                                      in_=dr["bqkv"][2 * D:3 * D]), "LM2")
                k.mark("su_rowv", "LM2")
        k.sec("sync", ld2)

        def vesp2(vector):
            if SU2_PART < 2:
                return
            for i in range(len(rows) + (1 if SU2_N > 6 else 0)):
                k.need(vector, f"su_row{i}" if i < len(rows) else "su_rowv",
                       "LM2")
                if i >= 1:
                    k.need(vector, f"su_bcm{i - 1}", "TE")
                r = rowb[0:1, 0:1024]
                vector.tensor_copy(rspH[0:1, :], r)
                vector.tensor_tensor(rspL[0:1, :], r, rspH[0:1, :],
                                     ALU.subtract)
                vector.tensor_tensor(evt[0:1, 0:1024], r, rspH[0:1, :],
                                     ALU.subtract)
                vector.tensor_tensor(rsp2[0:1, :], evt[0:1, 0:1024],
                                     rspL[0:1, :], ALU.subtract)
                k.inc(vector.drain(), "VE")
                k.mark(f"su_rsp{i}", "VE")
        k.sec("vector", vesp2)

        def te(tensor):
            if SU2_PART < 3:
                return
            for i in range(len(rows) + (1 if SU2_N > 6 else 0)):
                k.need(tensor, f"su_rsp{i}", "VE")
                if i >= 1:
                    k.need(tensor, f"su_bcc{i - 1}", "AC")
                for nh in range(2):
                    for j, src in enumerate((rspH, rspL, rsp2)):
                        tensor.matmul(PS[nh][0:128, 0:512], sel0b[:, 0:128],
                                      src[:, nh * 512:(nh + 1) * 512],
                                      start=(j == 0), stop=(j == 2))
                k.inc(tensor.drain(), "TE")
                k.mark(f"su_bcm{i}", "TE")
        k.sec("tensor", te)

        def ac(scalar):
            if SU2_PART < 4 or SU2_PART == 6:
                return
            for i in range(len(rows) + (1 if SU2_N > 6 else 0)):
                dst = rows[i][1] if i < len(rows) else R["vbqT"]
                if SU2_PART == 5:
                    dst = R["evt2"]
                k.need(scalar, f"su_bcm{i}", "TE")
                for nh in range(2):
                    scalar.copy(dst[:, nh * 512:(nh + 1) * 512],
                                PS[nh][0:128, 0:512])
                k.inc(scalar.drain(), "AC")
                k.mark(f"su_bcc{i}", "AC")
        k.sec("scalar", ac)

        def ac6(vector):
            if SU2_PART != 6:
                return
            for i in range(len(rows) + (1 if SU2_N > 6 else 0)):
                dst = rows[i][1] if i < len(rows) else R["vbqT"]
                k.need(vector, f"su_bcm{i}", "TE")
                for nh in range(2):
                    vector.tensor_copy(dst[:, nh * 512:(nh + 1) * 512],
                                       PS[nh][0:128, 0:512])
                k.inc(vector.drain(), "VE")
                k.mark(f"su_bcc{i}", "VE")
        k.sec("vector", ac6)
        k.barrier("SU2")

    def ph_A1():
        with (nc.sbuf_tensor("wstg", [128, 8 * 512], F32, allow_name_mangling=True) as wstg,
              nc.sbuf_tensor("waH", [128, 8 * 1024], BF16, allow_name_mangling=True) as waH,
              nc.sbuf_tensor("waL", [128, 8 * 1024], BF16, allow_name_mangling=True) as waL,
              nc.sbuf_tensor("xts", [128, 4 * 1024], F32, allow_name_mangling=True) as xts,
              nc.sbuf_tensor("xn32", [128, 4 * 1024], F32, allow_name_mangling=True) as xn32,
              nc.sbuf_tensor("xnfH", [128, 8 * 512], BF16, allow_name_mangling=True) as xnfH,
              nc.sbuf_tensor("xnfL", [128, 8 * 512], BF16, allow_name_mangling=True) as xnfL):

            def stagew(src, dH, dL, tag, after=None):
                def lw(sync):
                    if after is not None:
                        k.need(sync, after, "VE")
                    for nh in range(2):
                        if nh == 1:
                            k.need(sync, f"ws_{tag}_0", "VE")
                        k.dinc(sync.dma_start(
                            out=wstg[:, :].rearrange("p (c f) -> p c f", f=512),
                            in_=dr[src][:, nh * 512:(nh + 1) * 512].rearrange(
                                "(c p) f -> p c f", p=128)), "LW")
                        k.mark(f"w_{tag}_{nh}", "LW")
                k.sec("sync", lw)

                def sp(vector):
                    for nh in range(2):
                        k.need(vector, f"w_{tag}_{nh}", "LW")
                        dhv = dH[:, :].rearrange("p (c f) -> p c f", f=1024)[
                            :, :, nh * 512:(nh + 1) * 512]
                        dlv = dL[:, :].rearrange("p (c f) -> p c f", f=1024)[
                            :, :, nh * 512:(nh + 1) * 512]
                        wv = wstg[:, :].rearrange("p (c f) -> p c f", f=512)
                        vector.tensor_copy(dhv, wv)
                        vector.tensor_tensor(dlv, wv, dhv, ALU.subtract)
                        k.inc(vector.drain(), "VE")
                        k.mark(f"ws_{tag}_{nh}", "VE")
                    k.inc(vector.drain(), "VE")
                    k.mark(f"ws_{tag}", "VE")
                k.sec("vector", sp)

            def make_xnf(ts, tag, prev_done):
                def xl(sync):
                    if prev_done is not None:
                        k.need(sync, prev_done[0], prev_done[1])
                    for tt in range(4):
                        k.dinc(sync.dma_start(
                            out=xts[:, tt * 1024:(tt + 1) * 1024],
                            in_=dr["x"][ts * 512 + tt * 128:
                                        ts * 512 + (tt + 1) * 128, :]), "LX")
                    k.mark(f"{tag}x{ts}", "LX")
                k.sec("sync", xl)

                def acw(scalar):
                    k.need(scalar, f"{tag}x{ts}", "LX")
                    k.inc(scalar.drain(), "AC")
                k.sec("scalar", acw)

                def vew(vector):
                    k.need(vector, f"{tag}x{ts}", "LX")
                    k.inc(vector.drain(), "VE")
                k.sec("vector", vew)

                ln_stats(f"{tag}l{ts}",
                         lambda i: xts[:, i * 1024:(i + 1) * 1024], 4,
                         prev=prev_done)

                def ve2(vector):
                    k.need(vector, f"{tag}l{ts}B", "VE")
                    for tt in range(4):
                        if NOLN:
                            vector.tensor_copy(
                                xn32[:, tt * 1024:(tt + 1) * 1024],
                                xts[:, tt * 1024:(tt + 1) * 1024])
                        else:
                            vector.tensor_scalar(
                                xn32[:, tt * 1024:(tt + 1) * 1024],
                                xts[:, tt * 1024:(tt + 1) * 1024],
                                stats[:, 12 + tt:13 + tt],
                                stats[:, 20 + tt:21 + tt],
                                ALU.mult, ALU.add)
                    k.inc(vector.drain(), "VE")
                    k.mark(f"{tag}xn{ts}", "VE")
                k.sec("vector", ve2)

                def te(tensor):
                    k.need(tensor, f"{tag}xn{ts}", "VE")
                    if prev_done is not None:
                        k.need(tensor, prev_done[0], prev_done[1])
                    for c in range(8):
                        if c >= 2:
                            k.need(tensor, f"{tag}tl{ts}_{c - 2}", "VE")
                        for tt in range(4):
                            tensor.transpose(
                                PS[c % 2][0:128, tt * 128:(tt + 1) * 128],
                                xn32[:, tt * 1024 + c * 128:
                                     tt * 1024 + (c + 1) * 128],
                                idf[:, :])
                        k.inc(tensor.drain(), "TE")
                        k.mark(f"{tag}tp{ts}_{c}", "TE")
                k.sec("tensor", te)

                def ac3(scalar):
                    for c in range(8):
                        k.need(scalar, f"{tag}tp{ts}_{c}", "TE")
                        scalar.copy(xnfH[:, c * 512:(c + 1) * 512],
                                    PS[c % 2][0:128, 0:512])
                        k.inc(scalar.drain(), "AC")
                        k.mark(f"{tag}th{ts}_{c}", "AC")
                k.sec("scalar", ac3)

                def ve3(vector):
                    for c in range(8):
                        k.need(vector, f"{tag}th{ts}_{c}", "AC")
                        vector.tensor_tensor(xnfL[:, c * 512:(c + 1) * 512],
                                             PS[c % 2][0:128, 0:512],
                                             xnfH[:, c * 512:(c + 1) * 512],
                                             ALU.subtract)
                        k.inc(vector.drain(), "VE")
                        k.mark(f"{tag}tl{ts}_{c}", "VE")
                k.sec("vector", ve3)

            def proj_fm(ts, wH, wL, outH, outL, biasc, tag, normcol, wtag,
                        prev_done, xnftag, store_gate=None):
                def te(tensor):
                    k.need(tensor, f"ws_{wtag}", "VE")
                    k.need(tensor, f"{xnftag}tl{ts}_7", "VE")
                    if prev_done is not None:
                        k.need(tensor, prev_done[0], prev_done[1])
                    for m in range(8):
                        ps = PS[2 + m % 2]
                        if m >= 2:
                            k.need(tensor, f"{tag}pl{ts}_{m - 2}", "VE")
                        for c in range(8):
                            lh = wH[:, c * 1024 + m * 128:c * 1024 + (m + 1) * 128]
                            ll = wL[:, c * 1024 + m * 128:c * 1024 + (m + 1) * 128]
                            rh = xnfH[:, c * 512:(c + 1) * 512]
                            rl = xnfL[:, c * 512:(c + 1) * 512]
                            tensor.matmul(ps[:, 0:512], lh, rh,
                                          start=(c == 0), stop=False)
                            tensor.matmul(ps[:, 0:512], lh, rl,
                                          start=False, stop=False)
                            tensor.matmul(ps[:, 0:512], ll, rh,
                                          start=False, stop=(c == 7))
                        k.inc(tensor.drain(), "TE")
                        k.mark(f"{tag}pm{ts}_{m}", "TE")
                        if normcol is not None and m >= 1:
                            k.need(tensor, f"{tag}sqn{ts}_{m - 1}", "AC")
                            tensor.matmul(PS[6][0:16, 0:512],
                                          hsum[:, (m - 1) * 16:m * 16],
                                          sqb[:, :], start=(m == 1), stop=False)
                            k.inc(tensor.drain(), "TE")
                            k.mark(f"{tag}nmm{ts}_{m - 1}", "TE")
                    if normcol is not None:
                        k.need(tensor, f"{tag}sqn{ts}_7", "AC")
                        tensor.matmul(PS[6][0:16, 0:512], hsum[:, 7 * 16:8 * 16],
                                      sqb[:, :], start=False, stop=True)
                        k.inc(tensor.drain(), "TE")
                        k.mark(f"{tag}nmm{ts}_7", "TE")
                k.sec("tensor", te)

                def ac(scalar):
                    for m in range(8):
                        k.need(scalar, f"{tag}pm{ts}_{m}", "TE")
                        if m == 0 and store_gate is not None:
                            k.need(scalar, store_gate[0], store_gate[1])
                        ps = PS[2 + m % 2]
                        if biasc is None:
                            scalar.copy(outH[:, m * 512:(m + 1) * 512],
                                        ps[:, 0:512])
                        else:
                            scalar.activation(
                                out=outH[:, m * 512:(m + 1) * 512],
                                in_=ps[:, 0:512], func=AF.Identity,
                                bias=bqkv[:, biasc + m:biasc + m + 1],
                                scale=1.0)
                        k.inc(scalar.drain(), "AC")
                        k.mark(f"{tag}ph{ts}_{m}", "AC")
                        if normcol is not None:
                            if m >= 1:
                                k.need(scalar, f"{tag}nmm{ts}_{m - 1}", "TE")
                            scalar.square(sqb[:, :],
                                          outH[:, m * 512:(m + 1) * 512])
                            k.inc(scalar.drain(), "AC")
                            k.mark(f"{tag}sqn{ts}_{m}", "AC")
                k.sec("scalar", ac)

                def ve(vector):
                    for m in range(8):
                        k.need(vector, f"{tag}ph{ts}_{m}", "AC")
                        ps = PS[2 + m % 2]
                        if biasc is None:
                            vector.tensor_tensor(outL[:, m * 512:(m + 1) * 512],
                                                 ps[:, 0:512],
                                                 outH[:, m * 512:(m + 1) * 512],
                                                 ALU.subtract)
                        else:
                            vector.scalar_tensor_tensor(
                                outL[:, m * 512:(m + 1) * 512], ps[:, 0:512],
                                bqkv[:, biasc + m:biasc + m + 1],
                                outH[:, m * 512:(m + 1) * 512],
                                ALU.add, ALU.subtract)
                        k.inc(vector.drain(), "VE")
                        k.mark(f"{tag}pl{ts}_{m}", "VE")
                    if normcol is not None:
                        k.need(vector, f"{tag}nmm{ts}_7", "TE")
                        vector.reduce_max(stats[0:16, 28:29], PS[6][0:16, 0:512],
                                          axis=AX.X)
                        vector.tensor_max(n2m[:, normcol:normcol + 1],
                                          n2m[:, normcol:normcol + 1],
                                          stats[0:16, 28:29])
                        k.inc(vector.drain(), "VE")
                    k.mark(f"{tag}done{ts}", "VE")
                k.sec("vector", ve)

            def store_fm(ts, srcH, srcL, dstH, dstL, tag, gsem):
                def ge(gpsimd):
                    k.need(gpsimd, f"{tag}done{ts}", "VE")
                    for src, dst in ((srcH, dstH), (srcL, dstL)):
                        k.dinc(gpsimd.dma_start(
                            out=dr[dst].rearrange("(m p) t -> p m t", p=128)[
                                :, :, ts * 512:(ts + 1) * 512],
                            in_=src[:, :].rearrange("p (m t) -> p m t", t=512)),
                            gsem)
                    k.mark(f"{tag}st{ts}", gsem)
                k.sec("gpsimd", ge)

            def proj_v(ts, wH, wL, dstH, dstL, vbias, tag, wtag, prev_done,
                       xnftag):
                def ve0(vector):
                    if prev_done is not None:
                        k.need(vector, prev_done[0], prev_done[1])
                    for tt in range(4):
                        vector.memset(vtH[:, tt * 1040:(tt + 1) * 1040]
                                      .rearrange("p (h s) -> p h s", s=65)
                                      [:, :, 64:65], 1.0)
                        vector.memset(vtL[:, tt * 1040:(tt + 1) * 1040]
                                      .rearrange("p (h s) -> p h s", s=65)
                                      [:, :, 64:65], 0.0)
                    k.inc(vector.drain(), "VE")
                    k.mark(f"{tag}pad{ts}", "VE")
                k.sec("vector", ve0)

                def te(tensor):
                    k.need(tensor, f"ws_{wtag}", "VE")
                    k.need(tensor, f"{xnftag}tl{ts}_7", "VE")
                    if prev_done is not None:
                        k.need(tensor, prev_done[0], prev_done[1])
                    for i in range(8):
                        tt, nh = i // 2, i % 2
                        ps = PS[2 + i % 2]
                        if i >= 2:
                            k.need(tensor, f"{tag}vl{ts}_{i - 2}", "VE")
                        for c in range(8):
                            lh = xnfH[:, c * 512 + tt * 128:
                                      c * 512 + (tt + 1) * 128]
                            llo = xnfL[:, c * 512 + tt * 128:
                                       c * 512 + (tt + 1) * 128]
                            rh = wH[:, c * 1024 + nh * 512:
                                    c * 1024 + (nh + 1) * 512]
                            rl = wL[:, c * 1024 + nh * 512:
                                    c * 1024 + (nh + 1) * 512]
                            tensor.matmul(ps[:, 0:512], lh, rh,
                                          start=(c == 0), stop=False)
                            tensor.matmul(ps[:, 0:512], lh, rl,
                                          start=False, stop=False)
                            tensor.matmul(ps[:, 0:512], llo, rh,
                                          start=False, stop=(c == 7))
                        k.inc(tensor.drain(), "TE")
                        k.mark(f"{tag}vm{ts}_{i}", "TE")
                k.sec("tensor", te)

                def ve(vector):
                    for i in range(8):
                        tt, nh = i // 2, i % 2
                        k.need(vector, f"{tag}vm{ts}_{i}", "TE")
                        k.need(vector, f"{tag}pad{ts}", "VE")
                        ps = PS[2 + i % 2]
                        src = ps[:, 0:512]
                        if vbias is not None:
                            vector.tensor_add(evt[:, 0:512], src,
                                              vbias[:, nh * 512:(nh + 1) * 512])
                            src = evt[:, 0:512]
                        dh = vtH[:, tt * 1040:(tt + 1) * 1040].rearrange(
                            "p (h s) -> p h s", s=65)[:, nh * 8:(nh + 1) * 8, 0:64]
                        dl = vtL[:, tt * 1040:(tt + 1) * 1040].rearrange(
                            "p (h s) -> p h s", s=65)[:, nh * 8:(nh + 1) * 8, 0:64]
                        srcr = src.rearrange("p (h s) -> p h s", s=64)
                        vector.tensor_copy(dh, srcr)
                        vector.tensor_tensor(dl, srcr, dh, ALU.subtract)
                        k.inc(vector.drain(), "VE")
                        k.mark(f"{tag}vl{ts}_{i}", "VE")
                k.sec("vector", ve)

                def ge(gpsimd):
                    k.need(gpsimd, f"{tag}vl{ts}_7", "VE")
                    for src, dst in ((vtH, dstH), (vtL, dstL)):
                        for tt in range(4):
                            t0 = ts * 512 + tt * 128
                            k.dinc(gpsimd.dma_start(
                                out=dr[dst][:, t0:t0 + 128, :]
                                .rearrange("h p s -> p h s"),
                                in_=src[:, tt * 1040:(tt + 1) * 1040]
                                .rearrange("p (h s) -> p h s", s=65)), "GSV")
                    k.mark(f"{tag}vst{ts}", "GSV")
                k.sec("gpsimd", ge)

            with (nc.sbuf_tensor("wbH", [128, 8 * 1024], BF16, allow_name_mangling=True) as wbH,
                  nc.sbuf_tensor("wbL", [128, 8 * 1024], BF16, allow_name_mangling=True) as wbL,
                  nc.sbuf_tensor("o1H", [128, 8 * 512], BF16, allow_name_mangling=True) as o1H,
                  nc.sbuf_tensor("o1L", [128, 8 * 512], BF16, allow_name_mangling=True) as o1L,
                  nc.sbuf_tensor("o2H", [128, 8 * 512], BF16, allow_name_mangling=True) as o2H,
                  nc.sbuf_tensor("o2L", [128, 8 * 512], BF16, allow_name_mangling=True) as o2L):
              stagew("wqT", waH, waL, "aq")
              stagew("wkT", wbH, wbL, "ak", after="ws_aq")
              for ts in range(8):
                  if A1_MODE == "xn":
                      prev = (f"Aadone{ts - 1}", "VE") if ts > 0 else None
                      make_xnf(ts, "Aa", prev)

                      def gex(gpsimd, ts=ts):
                          k.need(gpsimd, f"Aatl{ts}_7", "VE")
                          k.dinc(gpsimd.dma_start(
                              out=dr["xnbbuf"][ts * 128:(ts + 1) * 128, 0:32],
                              in_=stats[:, 0:32]), "GSQ")
                          for src, dst in ((xnfH, "qaH"), (xnfL, "qaL")):
                              k.dinc(gpsimd.dma_start(
                                  out=dr[dst].rearrange(
                                      "(m p) t -> p m t", p=128)[
                                      :, :, ts * 512:(ts + 1) * 512],
                                  in_=src[:, :].rearrange(
                                      "p (m t) -> p m t", t=512)), "GSQ")
                          k.mark(f"Aaxst{ts}", "GSQ")
                      k.sec("gpsimd", gex)

                      def vd(vector, ts=ts):
                          k.need(vector, f"Aaxst{ts}", "GSQ")
                          k.inc(vector.drain(), "VE")
                          k.mark(f"Aadone{ts}", "VE")
                      k.sec("vector", vd)
                      continue
                  prev = (f"Akdone{ts - 1}", "VE") if ts > 0 else None
                  make_xnf(ts, "Aa", prev)
                  pg = (f"Aqst{ts - 1}", "GSQ") if ts > 0 else None
                  proj_fm(ts, waH, waL, o1H, o1L, None, "Aq", 0, "aq",
                          (f"Akdone{ts - 1}", "VE") if ts > 0 else None,
                          "Aa", store_gate=pg)
                  store_fm(ts, o1H, o1L, "qaH", "qaL", "Aq", "GSQ")
                  pg2 = (f"Akst{ts - 1}", "GSK") if ts > 0 else None
                  proj_fm(ts, wbH, wbL, o2H, o2L, None, "Ak", 1, "ak",
                          (f"Aqdone{ts}", "VE"), "Aa", store_gate=pg2)
                  store_fm(ts, o2H, o2L, "kaH", "kaL", "Ak", "GSK")
            k.barrier("A1m")
            if A1_MODE not in ("p1", "full"):
                k.barrier("A1")
                return
            with (nc.sbuf_tensor("vtH", [128, 4 * 1040], BF16, allow_name_mangling=True) as vtH,
                  nc.sbuf_tensor("vtL", [128, 4 * 1040], BF16, allow_name_mangling=True) as vtL):
              if A1_MODE == "p1":
                  k.barrier("A1")
                  return
              stagew("wvT", waH, waL, "av")
              for ts in range(8):
                  prev = (f"Avvl{ts - 1}_7", "VE") if ts > 0 else None
                  make_xnf(ts, "Ab", prev)
                  pv_prev = (f"Avvst{ts - 1}", "GSV") if ts > 0 else None
                  proj_v(ts, waH, waL, "vaH", "vaL", None, "Av", "av", pv_prev,
                         "Ab")
        k.barrier("A1")

    def ph_bounds(stage):
        cq, ck = (0, 1) if stage == "A" else (2, 3)
        dst = negbA if stage == "A" else negbB
        tag = f"bn{stage}"

        def ve(vector):
            vector.tensor_mul(stats[0:16, 28:29], n2m[:, cq:cq + 1],
                              n2m[:, ck:ck + 1])
            vector.tensor_scalar(stats[0:16, 28:29], stats[0:16, 28:29],
                                 cst[0:16, 3:4], None, ALU.mult)
            k.inc(vector.drain(), "VE")
            k.mark(f"{tag}m", "VE")
        k.sec("vector", ve)

        def ac(scalar):
            k.need(scalar, f"{tag}m", "VE")
            scalar.sqrt(stats[0:16, 29:30], stats[0:16, 28:29])
            k.inc(scalar.drain(), "AC")
            k.mark(f"{tag}s", "AC")
        k.sec("scalar", ac)

        def ve2(vector):
            k.need(vector, f"{tag}s", "AC")
            vector.tensor_scalar(stats[0:16, 29:30], stats[0:16, 29:30],
                                 cst[0:16, 4:5], None, ALU.min)
            vector.tensor_scalar(stats[0:16, 29:30], stats[0:16, 29:30],
                                 cst[0:16, 0:1], None, ALU.mult)
            vector.tensor_scalar(brepb[:, :], zeros16[:, :],
                                 stats[0:16, 29:30], None, ALU.add)
            k.inc(vector.drain(), "VE")
            k.mark(f"{tag}b", "VE")
        k.sec("vector", ve2)

        def te(tensor):
            k.need(tensor, f"{tag}b", "VE")
            tensor.matmul(PS[0][0:128, 0:16], brepb[0:16, :], i16b[0:16, :],
                          start=True, stop=True)
            k.inc(tensor.drain(), "TE")
            k.mark(f"{tag}mm", "TE")
        k.sec("tensor", te)

        def ac2(scalar):
            k.need(scalar, f"{tag}mm", "TE")
            scalar.copy(dst[:, :], PS[0][0:128, 0:16])
            k.inc(scalar.drain(), "AC")
            k.mark(f"{tag}done", "AC")
        k.sec("scalar", ac2)

    def src_kt(qbg, cls):
        kt = qbg + cls - 2
        seg = qbg // 16
        halo = (qbg - seg * 16) + cls - 2 < 0
        if kt < 0:
            kt += 32
        return kt, halo, (haloA if seg == 0 else haloB)

    def ph_A2():
        with (nc.sbuf_tensor("khpH", [128, T4], BF16, allow_name_mangling=True) as khpH,
              nc.sbuf_tensor("khpL", [128, T4], BF16, allow_name_mangling=True) as khpL,
              nc.sbuf_tensor("qhpH", [128, T4], BF16, allow_name_mangling=True) as qhpH,
              nc.sbuf_tensor("qhpL", [128, T4], BF16, allow_name_mangling=True) as qhpL,
              nc.sbuf_tensor("vhpH", [128, 2 * 32 * 65], BF16, allow_name_mangling=True) as vhpH,
              nc.sbuf_tensor("vhpL", [128, 2 * 32 * 65], BF16, allow_name_mangling=True) as vhpL,
              nc.sbuf_tensor("p32", [128, 3 * 512], F32, allow_name_mangling=True) as p32,
              nc.sbuf_tensor("phH", [128, 3 * 512], BF16, allow_name_mangling=True) as phH,
              nc.sbuf_tensor("phL", [128, 3 * 512], BF16, allow_name_mangling=True) as phL,
              nc.sbuf_tensor("mloA", [128, 512], F32, allow_name_mangling=True) as mlo,
              nc.sbuf_tensor("mhiA", [128, 512], F32, allow_name_mangling=True) as mhi,
              nc.sbuf_tensor("ctevH", [128, 2 * 512], BF16, allow_name_mangling=True) as ctevH,
              nc.sbuf_tensor("ctevL", [128, 2 * 512], BF16, allow_name_mangling=True) as ctevL):

            def ldmask(sync):
                k.dinc(sync.dma_start(out=mlo[:, :], in_=dr["masklo4"][:, :]),
                       "LM2")
                k.dinc(sync.dma_start(out=mhi[:, :], in_=dr["maskhi4"][:, :]),
                       "LM2")
                k.mark("A2msk", "LM2")
            k.sec("sync", ldmask)

            units = [(qc, cls, hl) for qc in range(8) for cls in range(3)
                     for hl in range(2)]

            def emit_pv(tensor, hp, u, prev_bc):
                qc, cls, hl = units[u]
                k.need(tensor, f"A2sp{hp}_{u}", "VE")
                ct = PS[4 + hl]
                if cls == 0:
                    if qc > 0:
                        k.need(tensor, f"A2ev{hp}_{hl}_{qc - 1}", "VE")
                    elif hp > 0:
                        k.need(tensor, f"A2ev{hp - 1}_{hl}_7", "VE")
                for qb in range(4):
                    qbg = qc * 4 + qb
                    kt, _, _ = src_kt(qbg, cls)
                    lhH = vhpH[:, hl * 2080 + kt * 65:hl * 2080 + (kt + 1) * 65]
                    lhL = vhpL[:, hl * 2080 + kt * 65:hl * 2080 + (kt + 1) * 65]
                    rh = phH[:, (u % 3) * 512 + qb * 128:
                             (u % 3) * 512 + (qb + 1) * 128]
                    rl = phL[:, (u % 3) * 512 + qb * 128:
                             (u % 3) * 512 + (qb + 1) * 128]
                    o = ct[0:65, qb * 128:(qb + 1) * 128]
                    tensor.matmul(o, lhH, rh, start=(cls == 0 and qb == 0),
                                  stop=False)
                    tensor.matmul(o, lhH, rl, start=False, stop=False)
                    tensor.matmul(o, lhL, rh, start=False,
                                  stop=(cls == 2 and qb == 3))
                k.inc(tensor.drain(), "TE")
                k.mark(f"A2pv{hp}_{u}", "TE")
                if cls == 2:
                    k.need(tensor, f"A2rr{hp}_{hl}_{qc}", "VE")
                    if prev_bc[0] is not None:
                        k.need(tensor, prev_bc[0], "AC")
                    tensor.matmul(PS[6][0:64, 0:512], sel64b[:, 0:64],
                                  rrowH[:, 0:512], start=True, stop=False)
                    tensor.matmul(PS[6][0:64, 0:512], sel64b[:, 0:64],
                                  rrowL[:, 0:512], start=False, stop=True)
                    k.inc(tensor.drain(), "TE")
                    k.mark(f"A2bc{hp}_{hl}_{qc}", "TE")
                    prev_bc[0] = f"A2bcc{hp}_{hl}_{qc}"

            for hp in range(8):
                def ldhp(sync, hp=hp):
                    if hp > 0:
                        k.need(sync, f"A2hpdone{hp - 1}", "VE")
                    for src, dst in (("kaH", khpH), ("kaL", khpL),
                                     ("qaH", qhpH), ("qaL", qhpL)):
                        k.dinc(sync.dma_start(
                            out=dst[:, :],
                            in_=dr[src][hp * 128:(hp + 1) * 128, :]), "LK")
                    for src, dst in (("vaH", vhpH), ("vaL", vhpL)):
                        for hl in range(2):
                            k.dinc(sync.dma_start(
                                out=dst[:, hl * 2080:(hl + 1) * 2080]
                                .rearrange("p (kt s) -> p kt s", s=65),
                                in_=dr[src][2 * hp + hl].rearrange(
                                    "(kt p) s -> p kt s", p=128)), "LK")
                    k.mark(f"A2hp{hp}", "LK")
                k.sec("sync", ldhp)

                def te(tensor, hp=hp):
                    k.need(tensor, f"A2hp{hp}", "LK")
                    k.need(tensor, "bnAdone", "AC")
                    prev_bc = [None]
                    for u, (qc, cls, hl) in enumerate(units):
                        sc = PS[u % 3]
                        if u >= 3:
                            k.need(tensor, f"A2ex{hp}_{u - 3}", "AC")
                        elif hp > 0:
                            k.need(tensor, f"A2ex{hp - 1}_{45 + u}", "AC")
                        for qb in range(4):
                            qbg = qc * 4 + qb
                            kt, _, _ = src_kt(qbg, cls)
                            lh = khpH[hl * 64:(hl + 1) * 64,
                                      kt * 128:(kt + 1) * 128]
                            ll = khpL[hl * 64:(hl + 1) * 64,
                                      kt * 128:(kt + 1) * 128]
                            rh = qhpH[hl * 64:(hl + 1) * 64,
                                      qbg * 128:(qbg + 1) * 128]
                            rl = qhpL[hl * 64:(hl + 1) * 64,
                                      qbg * 128:(qbg + 1) * 128]
                            o = sc[:, qb * 128:(qb + 1) * 128]
                            tensor.matmul(o, lh, rh, start=(qb == 0), stop=False)
                            tensor.matmul(o, lh, rl, start=False, stop=False)
                            tensor.matmul(o, ll, rh, start=False, stop=(qb == 3))
                        k.inc(tensor.drain(), "TE")
                        k.mark(f"A2sc{hp}_{u}", "TE")
                        if u >= 2:
                            emit_pv(tensor, hp, u - 2, prev_bc)
                    emit_pv(tensor, hp, len(units) - 2, prev_bc)
                    emit_pv(tensor, hp, len(units) - 1, prev_bc)
                k.sec("tensor", te)

                def veall(vector, hp=hp):
                    k.need(vector, "A2msk", "LM2")
                    prev_bcv = None
                    for u, (qc, cls, hl) in enumerate(units):
                        sc = PS[u % 3]
                        k.need(vector, f"A2sc{hp}_{u}", "TE")
                        if cls == 0:
                            vector.tensor_tensor(sc[:, 0:512], sc[:, 0:512],
                                                 mhi[:, :], ALU.add)
                        elif cls == 2:
                            vector.tensor_tensor(sc[:, 0:512], sc[:, 0:512],
                                                 mlo[:, :], ALU.add)
                        for qb in range(4):
                            qbg = qc * 4 + qb
                            _, halo, flag = src_kt(qbg, cls)
                            if halo:
                                vector.tensor_scalar(
                                    sc[:, qb * 128:(qb + 1) * 128],
                                    sc[:, qb * 128:(qb + 1) * 128],
                                    flag[:, 0:1], None, ALU.add)
                        k.inc(vector.drain(), "VE")
                        k.mark(f"A2mk{hp}_{u}", "VE")
                        k.need(vector, f"A2ex{hp}_{u}", "AC")
                        if u >= 3:
                            k.need(vector, f"A2pv{hp}_{u - 3}", "TE")
                        elif hp > 0:
                            k.need(vector, f"A2pv{hp - 1}_{45 + u}", "TE")
                        sl = slice((u % 3) * 512, (u % 3 + 1) * 512)
                        vector.tensor_copy(phH[:, sl], p32[:, sl])
                        vector.tensor_tensor(phL[:, sl], p32[:, sl],
                                             phH[:, sl], ALU.subtract)
                        k.inc(vector.drain(), "VE")
                        k.mark(f"A2sp{hp}_{u}", "VE")
                        if cls == 2:
                            k.need(vector, f"A2pv{hp}_{u}", "TE")
                            if prev_bcv is not None:
                                k.need(vector, prev_bcv, "TE")
                            ct = PS[4 + hl]
                            vector.reciprocal(rrow[64:65, 0:512],
                                              ct[64:65, 0:512])
                            vector.tensor_copy(rrowH[64:65, 0:512],
                                               rrow[64:65, 0:512])
                            vector.tensor_tensor(rrowL[64:65, 0:512],
                                                 rrow[64:65, 0:512],
                                                 rrowH[64:65, 0:512],
                                                 ALU.subtract)
                            k.inc(vector.drain(), "VE")
                            k.mark(f"A2rr{hp}_{hl}_{qc}", "VE")
                            prev_bcv = f"A2bc{hp}_{hl}_{qc}"
                            k.need(vector, f"A2bcc{hp}_{hl}_{qc}", "AC")
                            if qc > 0:
                                k.need(vector, f"A2stp{hp}_{hl}_{qc - 1}",
                                       "GC0" if hl == 0 else "GC1")
                            vector.tensor_tensor(evt[0:64, 0:512],
                                                 ct[0:64, 0:512],
                                                 bcsb[0:64, 0:512], ALU.mult)
                            vector.drain()
                            vector.tensor_copy(
                                ctevH[0:64, hl * 512:(hl + 1) * 512],
                                evt[0:64, 0:512])
                            vector.tensor_tensor(
                                ctevL[0:64, hl * 512:(hl + 1) * 512],
                                evt[0:64, 0:512],
                                ctevH[0:64, hl * 512:(hl + 1) * 512],
                                ALU.subtract)
                            k.inc(vector.drain(), "VE")
                            k.mark(f"A2ev{hp}_{hl}_{qc}", "VE")
                k.sec("vector", veall)

                def acall(scalar, hp=hp):
                    for u, (qc, cls, hl) in enumerate(units):
                        h = 2 * hp + hl
                        sc = PS[u % 3]
                        k.need(scalar, f"A2mk{hp}_{u}", "VE")
                        if u >= 3:
                            k.need(scalar, f"A2sp{hp}_{u - 3}", "VE")
                        elif hp > 0:
                            k.need(scalar, f"A2sp{hp - 1}_{45 + u}", "VE")
                        scalar.activation(
                            out=p32[:, (u % 3) * 512:(u % 3 + 1) * 512],
                            in_=sc[:, 0:512], func=AF.Exp,
                            bias=negbA[:, h:h + 1], scale=0.125)
                        k.inc(scalar.drain(), "AC")
                        k.mark(f"A2ex{hp}_{u}", "AC")
                        if cls == 2:
                            k.need(scalar, f"A2bc{hp}_{hl}_{qc}", "TE")
                            if not (qc == 0 and hl == 0):
                                phl, pqc = (0, qc) if hl == 1 else (1, qc - 1)
                                k.need(scalar, f"A2ev{hp}_{phl}_{pqc}", "VE")
                            scalar.copy(bcsb[0:64, 0:512], PS[6][0:64, 0:512])
                            k.inc(scalar.drain(), "AC")
                            k.mark(f"A2bcc{hp}_{hl}_{qc}", "AC")
                k.sec("scalar", acall)

                def gest(gpsimd, hp=hp):
                    for qc in range(8):
                        for hl in range(2):
                            h = 2 * hp + hl
                            k.need(gpsimd, f"A2ev{hp}_{hl}_{qc}", "VE")
                            gs = "GC0" if hl == 0 else "GC1"
                            k.dinc(gpsimd.dma_start(
                                out=dr["ctxaH"][h * 64:(h + 1) * 64,
                                                qc * 512:(qc + 1) * 512],
                                in_=ctevH[0:64, hl * 512:(hl + 1) * 512]), gs)
                            k.dinc(gpsimd.dma_start(
                                out=dr["ctxaL"][h * 64:(h + 1) * 64,
                                                qc * 512:(qc + 1) * 512],
                                in_=ctevL[0:64, hl * 512:(hl + 1) * 512]), gs)
                            k.mark(f"A2stp{hp}_{hl}_{qc}", gs)
                    k.mark(f"A2st{hp}_0", "GC0")
                    k.mark(f"A2st{hp}_1", "GC1")
                k.sec("gpsimd", gest)

                def vedone(vector, hp=hp):
                    k.need(vector, f"A2st{hp}_0", "GC0")
                    k.need(vector, f"A2st{hp}_1", "GC1")
                    k.inc(vector.drain(), "VE")
                    k.mark(f"A2hpdone{hp}", "VE")
                k.sec("vector", vedone)
        k.barrier("A2")

    def stagew_g(src, col0, wstg, dH, dL, tag, gates=()):
        def lw(sync):
            for g in gates:
                k.need(sync, g[0], g[1])
            for nh in range(2):
                if nh == 1:
                    k.need(sync, f"ws_{tag}_0", "VE")
                k.dinc(sync.dma_start(
                    out=wstg[:, :].rearrange("p (c f) -> p c f", f=512),
                    in_=dr[src][:, col0 + nh * 512:col0 + (nh + 1) * 512]
                    .rearrange("(c p) f -> p c f", p=128)), "LW")
                k.mark(f"w_{tag}_{nh}", "LW")
        k.sec("sync", lw)

        def sp(vector):
            for nh in range(2):
                k.need(vector, f"w_{tag}_{nh}", "LW")
                dhv = dH[:, :].rearrange("p (c f) -> p c f", f=1024)[
                    :, :, nh * 512:(nh + 1) * 512]
                dlv = dL[:, :].rearrange("p (c f) -> p c f", f=1024)[
                    :, :, nh * 512:(nh + 1) * 512]
                wv = wstg[:, :].rearrange("p (c f) -> p c f", f=512)
                vector.tensor_copy(dhv, wv)
                vector.tensor_tensor(dlv, wv, dhv, ALU.subtract)
                k.inc(vector.drain(), "VE")
                k.mark(f"ws_{tag}_{nh}", "VE")
            k.inc(vector.drain(), "VE")
            k.mark(f"ws_{tag}", "VE")
        k.sec("vector", sp)

    def store_fm_g(ts, srcH, srcL, dstH, dstL, tag, gsem):
        def ge(gpsimd):
            k.need(gpsimd, f"{tag}done{ts}", "VE")
            for src, dst in ((srcH, dstH), (srcL, dstL)):
                k.dinc(gpsimd.dma_start(
                    out=dr[dst].rearrange("(m p) t -> p m t", p=128)[
                        :, :, ts * 512:(ts + 1) * 512],
                    in_=src[:, :].rearrange("p (m t) -> p m t", t=512)), gsem)
            k.mark(f"{tag}st{ts}", gsem)
        k.sec("gpsimd", ge)

    def ph_A3():
        with (nc.sbuf_tensor("wstg3", [128, 8 * 512], F32, allow_name_mangling=True) as wstg,
              nc.sbuf_tensor("woH3", [128, 8 * 1024], BF16, allow_name_mangling=True) as woH,
              nc.sbuf_tensor("woL3", [128, 8 * 1024], BF16, allow_name_mangling=True) as woL,
              nc.sbuf_tensor("cslH3", [128, 8 * 512], BF16, allow_name_mangling=True) as cslH,
              nc.sbuf_tensor("cslL3", [128, 8 * 512], BF16, allow_name_mangling=True) as cslL,
              nc.sbuf_tensor("xtn3", [128, 4 * 1024], F32, allow_name_mangling=True) as xtn,
              nc.sbuf_tensor("aox3", [128, 4 * 1024], F32, allow_name_mangling=True) as aox,
              nc.sbuf_tensor("x1t3", [128, 4 * 1024], F32, allow_name_mangling=True) as x1t,
              nc.sbuf_tensor("xlfH3", [128, 8 * 512], BF16, allow_name_mangling=True) as xlfH,
              nc.sbuf_tensor("xlfL3", [128, 8 * 512], BF16, allow_name_mangling=True) as xlfL):
            stagew_g("woT", 0, wstg, woH, woL, "a3w")
            for ts in range(8):
                def ld(sync, ts=ts):
                    if ts >= 1:
                        k.need(sync, f"A3pm{ts - 1}_7", "TE")
                    for src, dst in (("ctxaH", cslH), ("ctxaL", cslL)):
                        k.dinc(sync.dma_start(
                            out=dst[:, :].rearrange("p (c t) -> p c t", t=512),
                            in_=dr[src].rearrange("(c p) t -> p c t", p=128)[
                                :, :, ts * 512:(ts + 1) * 512]), "LK")
                    k.mark(f"A3c{ts}", "LK")
                    if ts >= 1:
                        k.need(sync, f"A3xg{ts - 1}", "VE")
                        if ts - 1 < 4:
                            k.need(sync, f"A3nst{ts - 1}", "GSK")
                    for tt in range(4):
                        k.dinc(sync.dma_start(
                            out=xtn[:, tt * 1024:(tt + 1) * 1024],
                            in_=dr["x"][ts * 512 + tt * 128:
                                        ts * 512 + (tt + 1) * 128, :]), "LX")
                    k.mark(f"A3x{ts}", "LX")
                k.sec("sync", ld)

                def te(tensor, ts=ts):
                    k.need(tensor, "ws_a3w", "VE")
                    k.need(tensor, f"A3c{ts}", "LK")
                    for i in range(8):
                        tt, nh = i // 2, i % 2
                        ps = PS[2 + i % 2]
                        if i >= 2:
                            k.need(tensor, f"A3pe{ts}_{i - 2}", "VE")
                        elif ts >= 1:
                            k.need(tensor, f"A3pe{ts - 1}_{6 + i}", "VE")
                        for c in range(8):
                            lh = cslH[:, c * 512 + tt * 128:c * 512 + (tt + 1) * 128]
                            ll = cslL[:, c * 512 + tt * 128:c * 512 + (tt + 1) * 128]
                            rh = woH[:, c * 1024 + nh * 512:c * 1024 + (nh + 1) * 512]
                            rl = woL[:, c * 1024 + nh * 512:c * 1024 + (nh + 1) * 512]
                            tensor.matmul(ps[:, 0:512], lh, rh,
                                          start=(c == 0), stop=False)
                            tensor.matmul(ps[:, 0:512], lh, rl,
                                          start=False, stop=False)
                            tensor.matmul(ps[:, 0:512], ll, rh,
                                          start=False, stop=(c == 7))
                        k.inc(tensor.drain(), "TE")
                        k.mark(f"A3pm{ts}_{i}", "TE")
                    k.need(tensor, f"A3xl{ts}", "VE")
                    for c in range(8):
                        if c >= 2:
                            k.need(tensor, f"A3fl{ts}_{c - 2}", "VE")
                        elif ts >= 1:
                            k.need(tensor, f"A3fl{ts - 1}_{6 + c}", "VE")
                        for tt in range(4):
                            tensor.transpose(
                                PS[c % 2][0:128, tt * 128:(tt + 1) * 128],
                                aox[:, tt * 1024 + c * 128:tt * 1024 + (c + 1) * 128],
                                idf[:, :])
                        k.inc(tensor.drain(), "TE")
                        k.mark(f"A3tp{ts}_{c}", "TE")
                k.sec("tensor", te)

                def ve1(vector, ts=ts):
                    for i in range(8):
                        k.need(vector, f"A3pm{ts}_{i}", "TE")
                        if i == 0 and ts >= 1:
                            k.need(vector, f"A3tp{ts - 1}_7", "TE")
                        tt, nh = i // 2, i % 2
                        vector.tensor_copy(
                            aox[:, tt * 1024 + nh * 512:tt * 1024 + (nh + 1) * 512],
                            PS[2 + i % 2][:, 0:512])
                        k.inc(vector.drain(), "VE")
                        k.mark(f"A3pe{ts}_{i}", "VE")
                k.sec("vector", ve1)

                def acg1(scalar, ts=ts):
                    k.need(scalar, f"A3pe{ts}_7", "VE")
                    if ts >= 1:
                        k.need(scalar, f"A3xg{ts - 1}", "VE")
                    k.inc(scalar.drain(), "AC")
                k.sec("scalar", acg1)
                ln_stats(f"A3a{ts}",
                         lambda i: aox[:, i * 1024:(i + 1) * 1024], 4)

                def ve2(vector, ts=ts):
                    if ts >= 1 and ts - 1 < 4:
                        k.need(vector, f"A3x1st{ts - 1}", "GSQ")
                    for tt in range(4):
                        vector.tensor_scalar(
                            x1t[:, tt * 1024:(tt + 1) * 1024],
                            aox[:, tt * 1024:(tt + 1) * 1024],
                            stats[:, 12 + tt:13 + tt],
                            stats[:, 20 + tt:21 + tt], ALU.mult, ALU.add)
                    vector.drain()
                    for tt in range(4):
                        vector.tensor_tensor(
                            x1t[:, tt * 1024:(tt + 1) * 1024],
                            x1t[:, tt * 1024:(tt + 1) * 1024],
                            xtn[:, tt * 1024:(tt + 1) * 1024], ALU.add)
                    k.inc(vector.drain(), "VE")
                    k.mark(f"A3x1{ts}", "VE")
                k.sec("vector", ve2)

                if ts < 4:
                    def gest(gpsimd, ts=ts):
                        k.need(gpsimd, f"A3x1{ts}", "VE")
                        for tt in range(4):
                            k.dinc(gpsimd.dma_start(
                                out=dr["x1buf"][ts * 512 + tt * 128:
                                                ts * 512 + (tt + 1) * 128, :],
                                in_=x1t[:, tt * 1024:(tt + 1) * 1024]), "GSQ")
                        k.mark(f"A3x1st{ts}", "GSQ")
                    k.sec("gpsimd", gest)

                def acg2(scalar, ts=ts):
                    k.need(scalar, f"A3x1{ts}", "VE")
                    k.inc(scalar.drain(), "AC")
                k.sec("scalar", acg2)
                ln_stats(f"A3b{ts}",
                         lambda i: x1t[:, i * 1024:(i + 1) * 1024], 4)

                def ve3(vector, ts=ts):
                    for tt in range(4):
                        vector.tensor_scalar(
                            xtn[:, tt * 1024:(tt + 1) * 1024],
                            x1t[:, tt * 1024:(tt + 1) * 1024],
                            stats[:, 12 + tt:13 + tt],
                            stats[:, 20 + tt:21 + tt], ALU.mult, ALU.add)
                    k.inc(vector.drain(), "VE")
                    k.mark(f"A3nb{ts}", "VE")
                k.sec("vector", ve3)

                if ts < 4:
                    def gen(gpsimd, ts=ts):
                        k.need(gpsimd, f"A3nb{ts}", "VE")
                        for tt in range(4):
                            k.dinc(gpsimd.dma_start(
                                out=dr["xnbbuf"][ts * 512 + tt * 128:
                                                 ts * 512 + (tt + 1) * 128, :],
                                in_=xtn[:, tt * 1024:(tt + 1) * 1024]), "GSK")
                        k.mark(f"A3nst{ts}", "GSK")
                    k.sec("gpsimd", gen)

                def acg3(scalar, ts=ts):
                    k.need(scalar, f"A3nb{ts}", "VE")
                    k.inc(scalar.drain(), "AC")
                k.sec("scalar", acg3)
                ln_stats(f"A3c{ts}",
                         lambda i: xtn[:, i * 1024:(i + 1) * 1024], 4)

                def ve4(vector, ts=ts):
                    for tt in range(4):
                        vector.tensor_scalar(
                            aox[:, tt * 1024:(tt + 1) * 1024],
                            xtn[:, tt * 1024:(tt + 1) * 1024],
                            stats[:, 12 + tt:13 + tt],
                            stats[:, 20 + tt:21 + tt], ALU.mult, ALU.add)
                    k.inc(vector.drain(), "VE")
                    k.mark(f"A3xg{ts}", "VE")
                    for tt in range(4):
                        vector.tensor_tensor(aox[:, tt * 1024:(tt + 1) * 1024],
                                             aox[:, tt * 1024:(tt + 1) * 1024],
                                             g1T[:, :], ALU.mult)
                    vector.drain()
                    for tt in range(4):
                        vector.tensor_tensor(aox[:, tt * 1024:(tt + 1) * 1024],
                                             aox[:, tt * 1024:(tt + 1) * 1024],
                                             b1T[:, :], ALU.add)
                    k.inc(vector.drain(), "VE")
                    k.mark(f"A3xl{ts}", "VE")
                k.sec("vector", ve4)

                def ac3(scalar, ts=ts):
                    for c in range(8):
                        k.need(scalar, f"A3tp{ts}_{c}", "TE")
                        if c == 0 and ts >= 1:
                            k.need(scalar, f"A3fst{ts - 1}", "GSV")
                        scalar.copy(xlfH[:, c * 512:(c + 1) * 512],
                                    PS[c % 2][0:128, 0:512])
                        k.inc(scalar.drain(), "AC")
                        k.mark(f"A3fh{ts}_{c}", "AC")
                k.sec("scalar", ac3)

                def ve5(vector, ts=ts):
                    for c in range(8):
                        k.need(vector, f"A3fh{ts}_{c}", "AC")
                        vector.tensor_tensor(xlfL[:, c * 512:(c + 1) * 512],
                                             PS[c % 2][0:128, 0:512],
                                             xlfH[:, c * 512:(c + 1) * 512],
                                             ALU.subtract)
                        k.inc(vector.drain(), "VE")
                        k.mark(f"A3fl{ts}_{c}", "VE")
                k.sec("vector", ve5)

                def ge2(gpsimd, ts=ts):
                    k.need(gpsimd, f"A3fl{ts}_7", "VE")
                    for src, dst in ((xlfH, "xlfmH"), (xlfL, "xlfmL")):
                        k.dinc(gpsimd.dma_start(
                            out=dr[dst].rearrange("(m p) t -> p m t", p=128)[
                                :, :, ts * 512:(ts + 1) * 512],
                            in_=src[:, :].rearrange("p (m t) -> p m t", t=512)),
                            "GSV")
                    k.mark(f"A3fst{ts}", "GSV")
                k.sec("gpsimd", ge2)
        k.barrier("A3")

    def ph_B1():
        with (nc.sbuf_tensor("wstgb1", [128, 8 * 512], F32, allow_name_mangling=True) as wstg,
              nc.sbuf_tensor("wb1H", [128, 8 * 1024], BF16, allow_name_mangling=True) as wH,
              nc.sbuf_tensor("wb1L", [128, 8 * 1024], BF16, allow_name_mangling=True) as wL,
              nc.sbuf_tensor("xslH1", [128, 8 * 512], BF16, allow_name_mangling=True) as xslH,
              nc.sbuf_tensor("xslL1", [128, 8 * 512], BF16, allow_name_mangling=True) as xslL,
              nc.sbuf_tensor("ob1H", [128, 8 * 512], BF16, allow_name_mangling=True) as oH,
              nc.sbuf_tensor("ob1L", [128, 8 * 512], BF16, allow_name_mangling=True) as oL,
              nc.sbuf_tensor("vtb1H", [128, 4 * 1040], BF16, allow_name_mangling=True) as vtH,
              nc.sbuf_tensor("vtb1L", [128, 4 * 1040], BF16, allow_name_mangling=True) as vtL):

            def load_xsl(ts, tag, gate):
                def ld(sync):
                    if gate is not None:
                        k.need(sync, gate[0], gate[1])
                    for src, dst in (("xlfmH", xslH), ("xlfmL", xslL)):
                        k.dinc(sync.dma_start(
                            out=dst[:, :].rearrange("p (c t) -> p c t", t=512),
                            in_=dr[src].rearrange("(c p) t -> p c t", p=128)[
                                :, :, ts * 512:(ts + 1) * 512]), "LK")
                    k.mark(f"{tag}x{ts}", "LK")
                k.sec("sync", ld)

            def proj_fm_b(ts, biasc, tag, normcol, wtag, prev_done,
                          store_gate=None):
                def te(tensor):
                    k.need(tensor, f"ws_{wtag}", "VE")
                    k.need(tensor, f"{tag}x{ts}", "LK")
                    if prev_done is not None:
                        k.need(tensor, prev_done[0], prev_done[1])
                    for m in range(8):
                        ps = PS[2 + m % 2]
                        if m >= 2:
                            k.need(tensor, f"{tag}pl{ts}_{m - 2}", "VE")
                        for c in range(8):
                            lh = wH[:, c * 1024 + m * 128:c * 1024 + (m + 1) * 128]
                            ll = wL[:, c * 1024 + m * 128:c * 1024 + (m + 1) * 128]
                            rh = xslH[:, c * 512:(c + 1) * 512]
                            rl = xslL[:, c * 512:(c + 1) * 512]
                            tensor.matmul(ps[:, 0:512], lh, rh,
                                          start=(c == 0), stop=False)
                            tensor.matmul(ps[:, 0:512], lh, rl,
                                          start=False, stop=False)
                            tensor.matmul(ps[:, 0:512], ll, rh,
                                          start=False, stop=(c == 7))
                        k.inc(tensor.drain(), "TE")
                        k.mark(f"{tag}pm{ts}_{m}", "TE")
                        if normcol is not None and m >= 1:
                            k.need(tensor, f"{tag}sqn{ts}_{m - 1}", "AC")
                            tensor.matmul(PS[6][0:16, 0:512],
                                          hsum[:, (m - 1) * 16:m * 16],
                                          sqb[:, :], start=(m == 1), stop=False)
                            k.inc(tensor.drain(), "TE")
                            k.mark(f"{tag}nmm{ts}_{m - 1}", "TE")
                    if normcol is not None:
                        k.need(tensor, f"{tag}sqn{ts}_7", "AC")
                        tensor.matmul(PS[6][0:16, 0:512], hsum[:, 7 * 16:8 * 16],
                                      sqb[:, :], start=False, stop=True)
                        k.inc(tensor.drain(), "TE")
                        k.mark(f"{tag}nmm{ts}_7", "TE")
                k.sec("tensor", te)

                def ac(scalar):
                    for m in range(8):
                        k.need(scalar, f"{tag}pm{ts}_{m}", "TE")
                        if m == 0 and store_gate is not None:
                            k.need(scalar, store_gate[0], store_gate[1])
                        ps = PS[2 + m % 2]
                        scalar.activation(
                            out=oH[:, m * 512:(m + 1) * 512],
                            in_=ps[:, 0:512], func=AF.Identity,
                            bias=bqkv[:, biasc + m:biasc + m + 1], scale=1.0)
                        k.inc(scalar.drain(), "AC")
                        k.mark(f"{tag}ph{ts}_{m}", "AC")
                        if normcol is not None:
                            if m >= 1:
                                k.need(scalar, f"{tag}nmm{ts}_{m - 1}", "TE")
                            scalar.square(sqb[:, :],
                                          oH[:, m * 512:(m + 1) * 512])
                            k.inc(scalar.drain(), "AC")
                            k.mark(f"{tag}sqn{ts}_{m}", "AC")
                k.sec("scalar", ac)

                def ve(vector):
                    for m in range(8):
                        k.need(vector, f"{tag}ph{ts}_{m}", "AC")
                        ps = PS[2 + m % 2]
                        vector.scalar_tensor_tensor(
                            oL[:, m * 512:(m + 1) * 512], ps[:, 0:512],
                            bqkv[:, biasc + m:biasc + m + 1],
                            oH[:, m * 512:(m + 1) * 512],
                            ALU.add, ALU.subtract)
                        k.inc(vector.drain(), "VE")
                        k.mark(f"{tag}pl{ts}_{m}", "VE")
                    if normcol is not None:
                        k.need(vector, f"{tag}nmm{ts}_7", "TE")
                        vector.reduce_max(stats[0:16, 28:29], PS[6][0:16, 0:512],
                                          axis=AX.X)
                        vector.tensor_max(n2m[:, normcol:normcol + 1],
                                          n2m[:, normcol:normcol + 1],
                                          stats[0:16, 28:29])
                        k.inc(vector.drain(), "VE")
                    k.mark(f"{tag}done{ts}", "VE")
                k.sec("vector", ve)

            def proj_v_b(ts, tag, wtag, prev_done):
                def ve0(vector):
                    if prev_done is not None:
                        k.need(vector, prev_done[0], prev_done[1])
                    for tt in range(4):
                        vector.memset(vtH[:, tt * 1040:(tt + 1) * 1040]
                                      .rearrange("p (h s) -> p h s", s=65)
                                      [:, :, 64:65], 1.0)
                        vector.memset(vtL[:, tt * 1040:(tt + 1) * 1040]
                                      .rearrange("p (h s) -> p h s", s=65)
                                      [:, :, 64:65], 0.0)
                    k.inc(vector.drain(), "VE")
                    k.mark(f"{tag}pad{ts}", "VE")
                k.sec("vector", ve0)

                def te(tensor):
                    k.need(tensor, f"ws_{wtag}", "VE")
                    k.need(tensor, f"{tag}x{ts}", "LK")
                    if prev_done is not None:
                        k.need(tensor, prev_done[0], prev_done[1])
                    for i in range(8):
                        tt, nh = i // 2, i % 2
                        ps = PS[2 + i % 2]
                        if i >= 2:
                            k.need(tensor, f"{tag}vl{ts}_{i - 2}", "VE")
                        for c in range(8):
                            lh = xslH[:, c * 512 + tt * 128:
                                      c * 512 + (tt + 1) * 128]
                            llo = xslL[:, c * 512 + tt * 128:
                                       c * 512 + (tt + 1) * 128]
                            rh = wH[:, c * 1024 + nh * 512:
                                    c * 1024 + (nh + 1) * 512]
                            rl = wL[:, c * 1024 + nh * 512:
                                    c * 1024 + (nh + 1) * 512]
                            tensor.matmul(ps[:, 0:512], lh, rh,
                                          start=(c == 0), stop=False)
                            tensor.matmul(ps[:, 0:512], lh, rl,
                                          start=False, stop=False)
                            tensor.matmul(ps[:, 0:512], llo, rh,
                                          start=False, stop=(c == 7))
                        k.inc(tensor.drain(), "TE")
                        k.mark(f"{tag}vm{ts}_{i}", "TE")
                k.sec("tensor", te)

                def ve(vector):
                    for i in range(8):
                        tt, nh = i // 2, i % 2
                        k.need(vector, f"{tag}vm{ts}_{i}", "TE")
                        k.need(vector, f"{tag}pad{ts}", "VE")
                        ps = PS[2 + i % 2]
                        vector.tensor_add(evt[:, 0:512], ps[:, 0:512],
                                          vbqT[:, nh * 512:(nh + 1) * 512])
                        vector.drain()
                        src = evt[:, 0:512]
                        dh = vtH[:, tt * 1040:(tt + 1) * 1040].rearrange(
                            "p (h s) -> p h s", s=65)[:, nh * 8:(nh + 1) * 8, 0:64]
                        dl = vtL[:, tt * 1040:(tt + 1) * 1040].rearrange(
                            "p (h s) -> p h s", s=65)[:, nh * 8:(nh + 1) * 8, 0:64]
                        srcr = src.rearrange("p (h s) -> p h s", s=64)
                        vector.tensor_copy(dh, srcr)
                        vector.tensor_tensor(dl, srcr, dh, ALU.subtract)
                        k.inc(vector.drain(), "VE")
                        k.mark(f"{tag}vl{ts}_{i}", "VE")
                k.sec("vector", ve)

                def ge(gpsimd):
                    k.need(gpsimd, f"{tag}vl{ts}_7", "VE")
                    for src, dst in ((vtH, "vbH"), (vtL, "vbL")):
                        for tt in range(4):
                            t0 = ts * 512 + tt * 128
                            k.dinc(gpsimd.dma_start(
                                out=dr[dst][:, t0:t0 + 128, :]
                                .rearrange("h p s -> p h s"),
                                in_=src[:, tt * 1040:(tt + 1) * 1040]
                                .rearrange("p (h s) -> p h s", s=65)), "GSV")
                    k.mark(f"{tag}vst{ts}", "GSV")
                k.sec("gpsimd", ge)

            stagew_g("wqkv", 0, wstg, wH, wL, "b1q")
            for ts in range(4):
                load_xsl(ts, "Bq", (f"Bqpm{ts - 1}_7", "TE") if ts else None)
                proj_fm_b(ts, 0, "Bq", 2, "b1q",
                          (f"Bqdone{ts - 1}", "VE") if ts else None,
                          (f"Bqst{ts - 1}", "GSQ") if ts else None)
                store_fm_g(ts, oH, oL, "qaH", "qaL", "Bq", "GSQ")
            stagew_g("wqkv", D, wstg, wH, wL, "b1k",
                     gates=((f"Bqpm3_7", "TE"),))
            for ts in range(8):
                load_xsl(ts, "Bk", (f"Bkpm{ts - 1}_7", "TE") if ts
                         else (f"Bqpm3_7", "TE"))
                proj_fm_b(ts, 8, "Bk", 3, "b1k",
                          (f"Bkdone{ts - 1}", "VE") if ts else ("Bqdone3", "VE"),
                          (f"Bkst{ts - 1}", "GSK") if ts else ("Bqst3", "GSQ"))
                store_fm_g(ts, oH, oL, "kbH", "kbL", "Bk", "GSK")
            stagew_g("wqkv", 2 * D, wstg, wH, wL, "b1v",
                     gates=((f"Bkpm7_7", "TE"),))
            for ts in range(8):
                load_xsl(ts, "Bv", (f"Bvvm{ts - 1}_7", "TE") if ts
                         else (f"Bkpm7_7", "TE"))
                proj_v_b(ts, "Bv", "b1v",
                         (f"Bvvst{ts - 1}", "GSV") if ts else ("Bkdone7", "VE"))
        k.barrier("B1")

    def ph_B2():
        with (nc.sbuf_tensor("khpH2", [128, T4], BF16, allow_name_mangling=True) as khpH,
              nc.sbuf_tensor("khpL2", [128, T4], BF16, allow_name_mangling=True) as khpL,
              nc.sbuf_tensor("qhpH2", [128, T2], BF16, allow_name_mangling=True) as qhpH,
              nc.sbuf_tensor("qhpL2", [128, T2], BF16, allow_name_mangling=True) as qhpL,
              nc.sbuf_tensor("vhpH2", [128, 2 * 32 * 65], BF16, allow_name_mangling=True) as vhpH,
              nc.sbuf_tensor("vhpL2", [128, 2 * 32 * 65], BF16, allow_name_mangling=True) as vhpL,
              nc.sbuf_tensor("p32b", [128, 3 * 512], F32, allow_name_mangling=True) as p32,
              nc.sbuf_tensor("phHb", [128, 3 * 512], BF16, allow_name_mangling=True) as phH,
              nc.sbuf_tensor("phLb", [128, 3 * 512], BF16, allow_name_mangling=True) as phL,
              nc.sbuf_tensor("ctevHb", [128, 2 * 512], BF16, allow_name_mangling=True) as ctevH,
              nc.sbuf_tensor("ctevLb", [128, 2 * 512], BF16, allow_name_mangling=True) as ctevL):

            NG = 256

            def uhq(g):
                u = g // 32
                return u, u // 4, u % 4, g % 32

            for hp in range(8):
                def ldhp(sync, hp=hp):
                    if hp > 0:
                        k.need(sync, f"B2hpdone{hp - 1}", "VE")
                    for src, dst in (("kbH", khpH), ("kbL", khpL)):
                        k.dinc(sync.dma_start(
                            out=dst[:, :],
                            in_=dr[src][hp * 128:(hp + 1) * 128, :]), "LK")
                    for src, dst in (("qaH", qhpH), ("qaL", qhpL)):
                        k.dinc(sync.dma_start(
                            out=dst[:, :],
                            in_=dr[src][hp * 128:(hp + 1) * 128, 0:T2]), "LK")
                    for src, dst in (("vbH", vhpH), ("vbL", vhpL)):
                        for hl in range(2):
                            k.dinc(sync.dma_start(
                                out=dst[:, hl * 2080:(hl + 1) * 2080]
                                .rearrange("p (kt s) -> p kt s", s=65),
                                in_=dr[src][2 * hp + hl].rearrange(
                                    "(kt p) s -> p kt s", p=128)), "LK")
                    k.mark(f"B2hp{hp}", "LK")
                k.sec("sync", ldhp)

                def emit_pv2(tensor, hp, g):
                    u, hl, qc, kt = uhq(g)
                    k.need(tensor, f"B2sp{hp}_{g}", "VE")
                    ct = PS[4 + u % 2]
                    if kt == 0:
                        if u >= 2:
                            k.need(tensor, f"B2ev{hp}_{u - 2}", "VE")
                        elif hp > 0:
                            k.need(tensor, f"B2ev{hp - 1}_{6 + u}", "VE")
                    lhH = vhpH[:, hl * 2080 + kt * 65:hl * 2080 + (kt + 1) * 65]
                    lhL = vhpL[:, hl * 2080 + kt * 65:hl * 2080 + (kt + 1) * 65]
                    rh = phH[:, (g % 3) * 512:(g % 3 + 1) * 512]
                    rl = phL[:, (g % 3) * 512:(g % 3 + 1) * 512]
                    o = ct[0:65, 0:512]
                    tensor.matmul(o, lhH, rh, start=(kt == 0), stop=False)
                    tensor.matmul(o, lhH, rl, start=False, stop=False)
                    tensor.matmul(o, lhL, rh, start=False, stop=(kt == 31))
                    k.inc(tensor.drain(), "TE")
                    k.mark(f"B2pv{hp}_{g}", "TE")
                    if kt == 31:
                        k.need(tensor, f"B2rr{hp}_{u}", "VE")
                        if u > 0:
                            k.need(tensor, f"B2bcc{hp}_{u - 1}", "AC")
                        elif hp > 0:
                            k.need(tensor, f"B2bcc{hp - 1}_7", "AC")
                        tensor.matmul(PS[6][0:64, 0:512], sel64b[:, 0:64],
                                      rrowH[:, 0:512], start=True, stop=False)
                        tensor.matmul(PS[6][0:64, 0:512], sel64b[:, 0:64],
                                      rrowL[:, 0:512], start=False, stop=True)
                        k.inc(tensor.drain(), "TE")
                        k.mark(f"B2bc{hp}_{u}", "TE")

                def te(tensor, hp=hp):
                    k.need(tensor, f"B2hp{hp}", "LK")
                    k.need(tensor, "bnBdone", "AC")
                    for g in range(NG):
                        u, hl, qc, kt = uhq(g)
                        sc = PS[g % 2]
                        if g >= 2:
                            k.need(tensor, f"B2ex{hp}_{g - 2}", "AC")
                        elif hp > 0:
                            k.need(tensor, f"B2ex{hp - 1}_{NG - 2 + g}", "AC")
                        lh = khpH[hl * 64:(hl + 1) * 64, kt * 128:(kt + 1) * 128]
                        ll = khpL[hl * 64:(hl + 1) * 64, kt * 128:(kt + 1) * 128]
                        rh = qhpH[hl * 64:(hl + 1) * 64, qc * 512:(qc + 1) * 512]
                        rl = qhpL[hl * 64:(hl + 1) * 64, qc * 512:(qc + 1) * 512]
                        o = sc[:, 0:512]
                        tensor.matmul(o, lh, rh, start=True, stop=False)
                        tensor.matmul(o, lh, rl, start=False, stop=False)
                        tensor.matmul(o, ll, rh, start=False, stop=True)
                        k.inc(tensor.drain(), "TE")
                        k.mark(f"B2sc{hp}_{g}", "TE")
                        if g >= 2:
                            emit_pv2(tensor, hp, g - 2)
                    emit_pv2(tensor, hp, NG - 2)
                    emit_pv2(tensor, hp, NG - 1)
                k.sec("tensor", te)

                def acall(scalar, hp=hp):
                    for g in range(NG):
                        u, hl, qc, kt = uhq(g)
                        h = 2 * hp + hl
                        sc = PS[g % 2]
                        k.need(scalar, f"B2sc{hp}_{g}", "TE")
                        if g >= 3:
                            k.need(scalar, f"B2sp{hp}_{g - 3}", "VE")
                        elif hp > 0:
                            k.need(scalar, f"B2sp{hp - 1}_{NG - 3 + g}", "VE")
                        scalar.activation(
                            out=p32[:, (g % 3) * 512:(g % 3 + 1) * 512],
                            in_=sc[:, 0:512], func=AF.Exp,
                            bias=negbB[:, h:h + 1], scale=0.125)
                        k.inc(scalar.drain(), "AC")
                        k.mark(f"B2ex{hp}_{g}", "AC")
                        if kt == 1 and u > 0:
                            uu = u - 1
                            k.need(scalar, f"B2bc{hp}_{uu}", "TE")
                            if uu > 0:
                                k.need(scalar, f"B2ev{hp}_{uu - 1}", "VE")
                            elif hp > 0:
                                k.need(scalar, f"B2ev{hp - 1}_7", "VE")
                            scalar.copy(bcsb[0:64, 0:512], PS[6][0:64, 0:512])
                            k.inc(scalar.drain(), "AC")
                            k.mark(f"B2bcc{hp}_{uu}", "AC")
                    k.need(scalar, f"B2bc{hp}_7", "TE")
                    k.need(scalar, f"B2ev{hp}_6", "VE")
                    scalar.copy(bcsb[0:64, 0:512], PS[6][0:64, 0:512])
                    k.inc(scalar.drain(), "AC")
                    k.mark(f"B2bcc{hp}_7", "AC")
                k.sec("scalar", acall)

                def veall(vector, hp=hp):
                    for g in range(NG):
                        u, hl, qc, kt = uhq(g)
                        k.need(vector, f"B2ex{hp}_{g}", "AC")
                        if g >= 3:
                            k.need(vector, f"B2pv{hp}_{g - 3}", "TE")
                        elif hp > 0:
                            k.need(vector, f"B2pv{hp - 1}_{NG - 3 + g}", "TE")
                        sl = slice((g % 3) * 512, (g % 3 + 1) * 512)
                        vector.tensor_copy(phH[:, sl], p32[:, sl])
                        vector.tensor_tensor(phL[:, sl], p32[:, sl],
                                             phH[:, sl], ALU.subtract)
                        k.inc(vector.drain(), "VE")
                        k.mark(f"B2sp{hp}_{g}", "VE")
                        if kt == 31:
                            ct = PS[4 + u % 2]
                            k.need(vector, f"B2pv{hp}_{g}", "TE")
                            if u > 0:
                                k.need(vector, f"B2bc{hp}_{u - 1}", "TE")
                            elif hp > 0:
                                k.need(vector, f"B2bc{hp - 1}_7", "TE")
                            vector.reciprocal(rrow[64:65, 0:512],
                                              ct[64:65, 0:512])
                            vector.tensor_copy(rrowH[64:65, 0:512],
                                               rrow[64:65, 0:512])
                            vector.tensor_tensor(rrowL[64:65, 0:512],
                                                 rrow[64:65, 0:512],
                                                 rrowH[64:65, 0:512],
                                                 ALU.subtract)
                            k.inc(vector.drain(), "VE")
                            k.mark(f"B2rr{hp}_{u}", "VE")
                            k.need(vector, f"B2bcc{hp}_{u}", "AC")
                            if u >= 2:
                                k.need(vector, f"B2stp{hp}_{u - 2}",
                                       "GC0" if (u - 2) // 4 == 0 else "GC1")
                            elif hp > 0:
                                k.need(vector, f"B2stp{hp - 1}_{6 + u}",
                                       "GC0" if (6 + u) // 4 == 0 else "GC1")
                            vector.tensor_tensor(evt[0:64, 0:512],
                                                 ct[0:64, 0:512],
                                                 bcsb[0:64, 0:512], ALU.mult)
                            vector.drain()
                            sl2 = slice((u % 2) * 512, (u % 2 + 1) * 512)
                            vector.tensor_copy(ctevH[0:64, sl2],
                                               evt[0:64, 0:512])
                            vector.tensor_tensor(ctevL[0:64, sl2],
                                                 evt[0:64, 0:512],
                                                 ctevH[0:64, sl2],
                                                 ALU.subtract)
                            k.inc(vector.drain(), "VE")
                            k.mark(f"B2ev{hp}_{u}", "VE")
                k.sec("vector", veall)

                def gest(gpsimd, hp=hp):
                    for u in range(8):
                        hl, qc = u // 4, u % 4
                        h = 2 * hp + hl
                        k.need(gpsimd, f"B2ev{hp}_{u}", "VE")
                        gs = "GC0" if hl == 0 else "GC1"
                        sl2 = slice((u % 2) * 512, (u % 2 + 1) * 512)
                        k.dinc(gpsimd.dma_start(
                            out=dr["ctxbH"][h * 64:(h + 1) * 64,
                                            qc * 512:(qc + 1) * 512],
                            in_=ctevH[0:64, sl2]), gs)
                        k.dinc(gpsimd.dma_start(
                            out=dr["ctxbL"][h * 64:(h + 1) * 64,
                                            qc * 512:(qc + 1) * 512],
                            in_=ctevL[0:64, sl2]), gs)
                        k.mark(f"B2stp{hp}_{u}", gs)
                    k.mark(f"B2st{hp}_0", "GC0")
                    k.mark(f"B2st{hp}_1", "GC1")
                k.sec("gpsimd", gest)

                def vedone(vector, hp=hp):
                    k.need(vector, f"B2st{hp}_0", "GC0")
                    k.need(vector, f"B2st{hp}_1", "GC1")
                    k.inc(vector.drain(), "VE")
                    k.mark(f"B2hpdone{hp}", "VE")
                k.sec("vector", vedone)
        k.barrier("B2")

    def ph_B3a():
        with (nc.sbuf_tensor("wstg3a", [128, 8 * 512], F32, allow_name_mangling=True) as wstg,
              nc.sbuf_tensor("wo3H", [128, 8 * 1024], BF16, allow_name_mangling=True) as wH,
              nc.sbuf_tensor("wo3L", [128, 8 * 1024], BF16, allow_name_mangling=True) as wL,
              nc.sbuf_tensor("cslH3a", [128, 8 * 512], BF16, allow_name_mangling=True) as cslH,
              nc.sbuf_tensor("cslL3a", [128, 8 * 512], BF16, allow_name_mangling=True) as cslL,
              nc.sbuf_tensor("xnb3a", [128, 4 * 1024], F32, allow_name_mangling=True) as xnb,
              nc.sbuf_tensor("xi3a", [128, 4 * 1024], F32, allow_name_mangling=True) as xi32):
            stagew_g("fwo", 0, wstg, wH, wL, "b3w")
            for ts in range(4):
                def ld(sync, ts=ts):
                    if ts >= 1:
                        k.need(sync, f"B3pm{ts - 1}_7", "TE")
                    for src, dst in (("ctxbH", cslH), ("ctxbL", cslL)):
                        k.dinc(sync.dma_start(
                            out=dst[:, :].rearrange("p (c t) -> p c t", t=512),
                            in_=dr[src].rearrange("(c p) t -> p c t", p=128)[
                                :, :, ts * 512:(ts + 1) * 512]), "LK")
                    k.mark(f"B3c{ts}", "LK")
                    if ts >= 1:
                        k.need(sync, f"B3xi{ts - 1}", "VE")
                    for tt in range(4):
                        k.dinc(sync.dma_start(
                            out=xnb[:, tt * 1024:(tt + 1) * 1024],
                            in_=dr["xnbbuf"][ts * 512 + tt * 128:
                                             ts * 512 + (tt + 1) * 128, :]),
                            "LX")
                    k.mark(f"B3n{ts}", "LX")
                k.sec("sync", ld)

                def te(tensor, ts=ts):
                    k.need(tensor, "ws_b3w", "VE")
                    k.need(tensor, f"B3c{ts}", "LK")
                    for i in range(8):
                        tt, nh = i // 2, i % 2
                        ps = PS[2 + i % 2]
                        if i >= 2:
                            k.need(tensor, f"B3pe{ts}_{i - 2}", "VE")
                        elif ts >= 1:
                            k.need(tensor, f"B3pe{ts - 1}_{6 + i}", "VE")
                        for c in range(8):
                            lh = cslH[:, c * 512 + tt * 128:c * 512 + (tt + 1) * 128]
                            ll = cslL[:, c * 512 + tt * 128:c * 512 + (tt + 1) * 128]
                            rh = wH[:, c * 1024 + nh * 512:c * 1024 + (nh + 1) * 512]
                            rl = wL[:, c * 1024 + nh * 512:c * 1024 + (nh + 1) * 512]
                            tensor.matmul(ps[:, 0:512], lh, rh,
                                          start=(c == 0), stop=False)
                            tensor.matmul(ps[:, 0:512], lh, rl,
                                          start=False, stop=False)
                            tensor.matmul(ps[:, 0:512], ll, rh,
                                          start=False, stop=(c == 7))
                        k.inc(tensor.drain(), "TE")
                        k.mark(f"B3pm{ts}_{i}", "TE")
                k.sec("tensor", te)

                def ve(vector, ts=ts):
                    k.need(vector, f"B3n{ts}", "LX")
                    for i in range(8):
                        tt, nh = i // 2, i % 2
                        k.need(vector, f"B3pm{ts}_{i}", "TE")
                        if i == 0 and ts >= 1:
                            k.need(vector, f"B3ist{ts - 1}", "GSQ")
                        sl = slice(tt * 1024 + nh * 512, tt * 1024 + (nh + 1) * 512)
                        vector.tensor_add(xi32[:, sl], PS[2 + i % 2][:, 0:512],
                                          boT[:, nh * 512:(nh + 1) * 512])
                        vector.drain()
                        vector.tensor_tensor(xi32[:, sl], xi32[:, sl],
                                             xnb[:, sl], ALU.add)
                        k.inc(vector.drain(), "VE")
                        k.mark(f"B3pe{ts}_{i}", "VE")
                    k.inc(vector.drain(), "VE")
                    k.mark(f"B3xi{ts}", "VE")
                k.sec("vector", ve)

                def ge(gpsimd, ts=ts):
                    k.need(gpsimd, f"B3xi{ts}", "VE")
                    for tt in range(4):
                        k.dinc(gpsimd.dma_start(
                            out=dr["xibuf"][ts * 512 + tt * 128:
                                            ts * 512 + (tt + 1) * 128, :],
                            in_=xi32[:, tt * 1024:(tt + 1) * 1024]), "GSQ")
                    k.mark(f"B3ist{ts}", "GSQ")
                k.sec("gpsimd", ge)
        k.barrier("B3a")

    def ph_B3b():
        with (nc.sbuf_tensor("xit3b", [128, 4 * 1024], F32, allow_name_mangling=True) as xits,
              nc.sbuf_tensor("xn23b", [128, 4 * 1024], F32, allow_name_mangling=True) as xn232,
              nc.sbuf_tensor("fmH3b", [128, 8 * 512], BF16, allow_name_mangling=True) as fmH,
              nc.sbuf_tensor("fmL3b", [128, 8 * 512], BF16, allow_name_mangling=True) as fmL):
            for ts in range(4):
                def ld(sync, ts=ts):
                    if ts >= 1:
                        k.need(sync, f"B3brs{ts - 1}", "VE")
                    for tt in range(4):
                        k.dinc(sync.dma_start(
                            out=xits[:, tt * 1024:(tt + 1) * 1024],
                            in_=dr["xibuf"][ts * 512 + tt * 128:
                                            ts * 512 + (tt + 1) * 128, :]),
                            "LX")
                    k.mark(f"B3bx{ts}", "LX")
                k.sec("sync", ld)

                def acg(scalar, ts=ts):
                    k.need(scalar, f"B3bx{ts}", "LX")
                    if ts >= 1:
                        k.need(scalar, f"B3brs{ts - 1}", "VE")
                    k.inc(scalar.drain(), "AC")
                k.sec("scalar", acg)

                def veg(vector, ts=ts):
                    k.need(vector, f"B3bx{ts}", "LX")
                    k.inc(vector.drain(), "VE")
                k.sec("vector", veg)
                ln_stats(f"B3l{ts}",
                         lambda i: xits[:, i * 1024:(i + 1) * 1024], 4)

                def ve(vector, ts=ts):
                    if ts >= 1:
                        k.need(vector, f"B3nst{ts - 1}", "GSQ")
                        k.need(vector, f"B3tp{ts - 1}_7", "TE")
                    for tt in range(4):
                        vector.tensor_scalar(
                            xn232[:, tt * 1024:(tt + 1) * 1024],
                            xits[:, tt * 1024:(tt + 1) * 1024],
                            stats[:, 12 + tt:13 + tt],
                            stats[:, 20 + tt:21 + tt], ALU.mult, ALU.add)
                    k.inc(vector.drain(), "VE")
                    k.mark(f"B3brs{ts}", "VE")
                    for tt in range(4):
                        vector.tensor_tensor(xn232[:, tt * 1024:(tt + 1) * 1024],
                                             xn232[:, tt * 1024:(tt + 1) * 1024],
                                             g2T[:, :], ALU.mult)
                    vector.drain()
                    for tt in range(4):
                        vector.tensor_tensor(xn232[:, tt * 1024:(tt + 1) * 1024],
                                             xn232[:, tt * 1024:(tt + 1) * 1024],
                                             b2T[:, :], ALU.add)
                    k.inc(vector.drain(), "VE")
                    k.mark(f"B3n2{ts}", "VE")
                k.sec("vector", ve)

                def ge(gpsimd, ts=ts):
                    k.need(gpsimd, f"B3n2{ts}", "VE")
                    for tt in range(4):
                        k.dinc(gpsimd.dma_start(
                            out=dr["xn2buf"][ts * 512 + tt * 128:
                                             ts * 512 + (tt + 1) * 128, :],
                            in_=xn232[:, tt * 1024:(tt + 1) * 1024]), "GSQ")
                    k.mark(f"B3nst{ts}", "GSQ")
                k.sec("gpsimd", ge)

                def te(tensor, ts=ts):
                    k.need(tensor, f"B3n2{ts}", "VE")
                    for c in range(8):
                        if c >= 2:
                            k.need(tensor, f"B3fl{ts}_{c - 2}", "VE")
                        elif ts >= 1:
                            k.need(tensor, f"B3fl{ts - 1}_{6 + c}", "VE")
                        for tt in range(4):
                            tensor.transpose(
                                PS[c % 2][0:128, tt * 128:(tt + 1) * 128],
                                xn232[:, tt * 1024 + c * 128:
                                      tt * 1024 + (c + 1) * 128],
                                idf[:, :])
                        k.inc(tensor.drain(), "TE")
                        k.mark(f"B3tp{ts}_{c}", "TE")
                k.sec("tensor", te)

                def ac2(scalar, ts=ts):
                    for c in range(8):
                        k.need(scalar, f"B3tp{ts}_{c}", "TE")
                        if c == 0 and ts >= 1:
                            k.need(scalar, f"B3fst{ts - 1}", "GSV")
                        scalar.copy(fmH[:, c * 512:(c + 1) * 512],
                                    PS[c % 2][0:128, 0:512])
                        k.inc(scalar.drain(), "AC")
                        k.mark(f"B3fh{ts}_{c}", "AC")
                k.sec("scalar", ac2)

                def ve2(vector, ts=ts):
                    for c in range(8):
                        k.need(vector, f"B3fh{ts}_{c}", "AC")
                        vector.tensor_tensor(fmL[:, c * 512:(c + 1) * 512],
                                             PS[c % 2][0:128, 0:512],
                                             fmH[:, c * 512:(c + 1) * 512],
                                             ALU.subtract)
                        k.inc(vector.drain(), "VE")
                        k.mark(f"B3fl{ts}_{c}", "VE")
                k.sec("vector", ve2)

                def ge2(gpsimd, ts=ts):
                    k.need(gpsimd, f"B3fl{ts}_7", "VE")
                    for src, dst in ((fmH, "xn2fmH"), (fmL, "xn2fmL")):
                        k.dinc(gpsimd.dma_start(
                            out=dr[dst].rearrange("(m p) t -> p m t", p=128)[
                                :, :, ts * 512:(ts + 1) * 512],
                            in_=src[:, :].rearrange("p (m t) -> p m t", t=512)),
                            "GSV")
                    k.mark(f"B3fst{ts}", "GSV")
                k.sec("gpsimd", ge2)
        k.barrier("B3b")

    def ph_B4():
        with (nc.sbuf_tensor("xslH4", [128, 8 * 512], BF16, allow_name_mangling=True) as xslH,
              nc.sbuf_tensor("xslL4", [128, 8 * 512], BF16, allow_name_mangling=True) as xslL,
              nc.sbuf_tensor("gexs", [128, 8], F32, allow_name_mangling=True) as gexs,
              nc.sbuf_tensor("b4oh", [128, 8], F32, allow_name_mangling=True) as b4oh,
              nc.sbuf_tensor("b4s", [128, 16], F32, allow_name_mangling=True) as b4s,
              nc.sbuf_tensor("x2t4", [128, 1024], F32, allow_name_mangling=True) as x2t,
              nc.sbuf_tensor("xscH4", [128, 1024], BF16, allow_name_mangling=True) as xscH,
              nc.sbuf_tensor("xscL4", [128, 1024], BF16, allow_name_mangling=True) as xscL):
            for ts in range(4):
                def ld(sync, ts=ts):
                    if ts >= 1:
                        k.need(sync, f"B4mm{ts - 1}_3", "TE")
                    for src, dst in (("xn2fmH", xslH), ("xn2fmL", xslL)):
                        k.dinc(sync.dma_start(
                            out=dst[:, :].rearrange("p (c t) -> p c t", t=512),
                            in_=dr[src].rearrange("(c p) t -> p c t", p=128)[
                                :, :, ts * 512:(ts + 1) * 512]), "LK")
                    k.mark(f"B4x{ts}", "LK")
                k.sec("sync", ld)

                def te(tensor, ts=ts):
                    k.need(tensor, f"B4x{ts}", "LK")
                    for tt in range(4):
                        t = ts * 4 + tt
                        if t >= 2:
                            k.need(tensor, f"B4pe{t - 2}", "VE")
                        gps = PS[tt % 2]
                        cps = PS[2 + tt % 2]
                        for c in range(8):
                            lh = xslH[:, c * 512 + tt * 128:c * 512 + (tt + 1) * 128]
                            ll = xslL[:, c * 512 + tt * 128:c * 512 + (tt + 1) * 128]
                            gh = gwH[:, c * 8:(c + 1) * 8]
                            gl = gwL[:, c * 8:(c + 1) * 8]
                            tensor.matmul(gps[:, 0:8], lh, gh,
                                          start=(c == 0), stop=False)
                            tensor.matmul(gps[:, 0:8], lh, gl,
                                          start=False, stop=False)
                            tensor.matmul(gps[:, 0:8], ll, gh,
                                          start=False, stop=(c == 7))
                            ch = cwH[:, c * 8:(c + 1) * 8]
                            cl = cwL[:, c * 8:(c + 1) * 8]
                            tensor.matmul(cps[:, 0:8], lh, ch,
                                          start=(c == 0), stop=False)
                            tensor.matmul(cps[:, 0:8], lh, cl,
                                          start=False, stop=False)
                            tensor.matmul(cps[:, 0:8], ll, ch,
                                          start=False, stop=False)
                        tensor.matmul(cps[:, 0:8], onesb[:, 0:128],
                                      cbH[:, :], start=False, stop=False)
                        tensor.matmul(cps[:, 0:8], onesb[:, 0:128],
                                      cbL[:, :], start=False, stop=True)
                        k.inc(tensor.drain(), "TE")
                        k.mark(f"B4mm{ts}_{tt}", "TE")
                k.sec("tensor", te)

                def ve(vector, ts=ts):
                    for tt in range(4):
                        t = ts * 4 + tt
                        gps = PS[tt % 2]
                        cps = PS[2 + tt % 2]
                        k.need(vector, f"B4mm{ts}_{tt}", "TE")
                        vector.reduce_max(b4s[:, 0:1], gps[:, 0:8], axis=AX.X)
                        vector.drain()
                        vector.tensor_scalar(b4s[:, 1:2], b4s[:, 0:1],
                                             cst[:, 0:1], None, ALU.mult)
                        k.inc(vector.drain(), "VE")
                        k.mark(f"B4gm{t}", "VE")
                        vector.tensor_scalar(b4oh[:, :], gps[:, 0:8],
                                             b4s[:, 0:1], None, ALU.is_equal)
                        vector.drain()
                        vector.tensor_mul(b4oh[:, :], b4oh[:, :], erowmB[:, :])
                        vector.drain()
                        vector.tensor_scalar(b4oh[:, :], b4oh[:, :],
                                             cst[:, 7:8], None, ALU.add,
                                             ALU.min, accum_out=idxf[:, t:t + 1])
                        vector.drain()
                        vector.tensor_scalar(oht[:, t * 8:(t + 1) * 8],
                                             erow[:, :], idxf[:, t:t + 1],
                                             None, ALU.is_equal)
                        k.inc(vector.drain(), "VE")
                        k.mark(f"B4oh{t}", "VE")
                        vector.tensor_copy(b4s[:, 13:15], cps[:, 0:2])
                        vector.drain()
                        vector.tensor_max(b4s[:, 6:7], b4s[:, 13:14],
                                          b4s[:, 14:15])
                        vector.drain()
                        vector.tensor_scalar(b4s[:, 7:8], b4s[:, 6:7],
                                             cst[:, 0:1], None, ALU.mult)
                        k.inc(vector.drain(), "VE")
                        k.mark(f"B4cm{t}", "VE")
                        k.need(vector, f"B4ce{t}", "AC")
                        vector.reciprocal(b4s[:, 3:4], b4s[:, 2:3])
                        vector.reciprocal(b4s[:, 9:10], b4s[:, 8:9])
                        vector.drain()
                        vector.tensor_copy(gvt[:, t:t + 1], b4s[:, 3:4])
                        vector.tensor_mul(c0t[:, t:t + 1], b4s[:, 4:5],
                                          b4s[:, 9:10])
                        vector.tensor_mul(c1t[:, t:t + 1], b4s[:, 5:6],
                                          b4s[:, 9:10])
                        k.inc(vector.drain(), "VE")
                        k.mark(f"B4pe{t}", "VE")
                k.sec("vector", ve)

                def ac(scalar, ts=ts):
                    for tt in range(4):
                        t = ts * 4 + tt
                        gps = PS[tt % 2]
                        cps = PS[2 + tt % 2]
                        k.need(scalar, f"B4gm{t}", "VE")
                        scalar.activation(out=gexs[:, 0:8], in_=gps[:, 0:8],
                                          func=AF.Exp, bias=b4s[:, 1:2],
                                          scale=1.0, accum_out=b4s[:, 2:3])
                        k.need(scalar, f"B4cm{t}", "VE")
                        scalar.activation(out=b4s[:, 4:6], in_=b4s[:, 13:15],
                                          func=AF.Exp, bias=b4s[:, 7:8],
                                          scale=1.0, accum_out=b4s[:, 8:9])
                        k.inc(scalar.drain(), "AC")
                        k.mark(f"B4ce{t}", "AC")
                k.sec("scalar", ac)

            def teb(tensor):
                for j in range(16):
                    k.need(tensor, f"B4oh{j}", "VE")
                    if j < 2:
                        k.need(tensor, f"B4pe{14 + j}", "VE")
                    else:
                        k.need(tensor, f"B4rk{j - 2}", "VE")
                    ps = PS[j % 2]
                    for i in range(j):
                        tensor.matmul(ps[:, 0:8], onesb[:, 0:128],
                                      oht[:, i * 8:(i + 1) * 8],
                                      start=(i == 0), stop=False)
                    tensor.matmul(ps[:, 0:8], trilb[:, 0:128],
                                  oht[:, j * 8:(j + 1) * 8],
                                  start=(j == 0), stop=True)
                    k.inc(tensor.drain(), "TE")
                    k.mark(f"B4pf{j}", "TE")
            k.sec("tensor", teb)

            def veb(vector):
                for j in range(16):
                    k.need(vector, f"B4pf{j}", "TE")
                    vector.tensor_mul(b4oh[:, :], PS[j % 2][:, 0:8],
                                      oht[:, j * 8:(j + 1) * 8])
                    vector.drain()
                    vector.tensor_scalar(b4oh[:, :], b4oh[:, :],
                                         cst[:, 9:10], None, ALU.mult,
                                         ALU.add, accum_out=b4s[:, 10:11])
                    vector.tensor_scalar(b4s[:, 11:12], idxf[:, j:j + 1],
                                         cst[:, 6:7], None, ALU.mult)
                    vector.drain()
                    vector.tensor_tensor(b4s[:, 12:13], b4s[:, 11:12],
                                         b4s[:, 10:11], ALU.add)
                    vector.drain()
                    vector.tensor_scalar(b4s[:, 12:13], b4s[:, 12:13],
                                         cst[:, 12:13], None, ALU.min)
                    vector.tensor_copy(dest32[:, j:j + 1], b4s[:, 12:13])
                    k.inc(vector.drain(), "VE")
                    k.mark(f"B4rk{j}", "VE")
            k.sec("vector", veb)

            for j in range(16):
                def lds(sync, j=j):
                    if j >= 1:
                        k.need(sync, f"B4sxs{j - 1}", "VE")
                    k.dinc(sync.dma_start(
                        out=x2t[:, :],
                        in_=dr["xn2buf"][j * 128:(j + 1) * 128, :]), "LX")
                    k.mark(f"B4xr{j}", "LX")
                k.sec("sync", lds)

                def ves(vector, j=j):
                    k.need(vector, f"B4xr{j}", "LX")
                    if j >= 1:
                        k.need(vector, f"B4sc{j - 1}", "GI")
                    vector.tensor_copy(xscH[:, :], x2t[:, :])
                    vector.tensor_tensor(xscL[:, :], x2t[:, :], xscH[:, :],
                                         ALU.subtract)
                    k.inc(vector.drain(), "VE")
                    k.mark(f"B4sxs{j}", "VE")
                k.sec("vector", ves)

                def ges(gpsimd, j=j):
                    k.need(gpsimd, "su_xez", "GI")
                    k.need(gpsimd, f"B4rk{j}", "VE")
                    k.need(gpsimd, f"B4sxs{j}", "VE")
                    k.dinc(gpsimd.indirect_dma_start(
                        out=dr["xeH"],
                        out_offset=IndirectOffsetOnAxis(dest32[:, j:j + 1],
                                                        axis=0),
                        in_=xscH[:, :], in_offset=None), "GI")
                    k.dinc(gpsimd.indirect_dma_start(
                        out=dr["xeL"],
                        out_offset=IndirectOffsetOnAxis(dest32[:, j:j + 1],
                                                        axis=0),
                        in_=xscL[:, :], in_offset=None), "GI")
                    k.mark(f"B4sc{j}", "GI")
                k.sec("gpsimd", ges)
        k.barrier("B4")

    def ph_B5():
        with (nc.sbuf_tensor("hfm5H", [128, 32 * 512], BF16, allow_name_mangling=True) as hfmH,
              nc.sbuf_tensor("hfm5L", [128, 32 * 512], BF16, allow_name_mangling=True) as hfmL,
              nc.sbuf_tensor("xslH5", [128, 8 * 512], BF16, allow_name_mangling=True) as xslH,
              nc.sbuf_tensor("xslL5", [128, 8 * 512], BF16, allow_name_mangling=True) as xslL,
              nc.sbuf_tensor("wstg5", [128, 8 * 512], F32, allow_name_mangling=True) as wstg,
              nc.sbuf_tensor("w5H", [128, 8 * 512], BF16, allow_name_mangling=True) as w5H,
              nc.sbuf_tensor("w5L", [128, 8 * 512], BF16, allow_name_mangling=True) as w5L,
              nc.sbuf_tensor("ws2d", [128, 2 * 512], F32, allow_name_mangling=True) as ws2,
              nc.sbuf_tensor("w2h5", [128, 2 * 512], BF16, allow_name_mangling=True) as w2h,
              nc.sbuf_tensor("w2l5", [128, 2 * 512], BF16, allow_name_mangling=True) as w2l,
              nc.sbuf_tensor("rms5", [128, 4 * 1024], F32, allow_name_mangling=True) as rms):
            for tp in range(4):
                def ldx(sync, tp=tp):
                    if tp >= 1:
                        k.need(sync, f"B5m1{tp - 1}_7", "TE")
                    for src, dst in (("xn2fmH", xslH), ("xn2fmL", xslL)):
                        k.dinc(sync.dma_start(
                            out=dst[:, :].rearrange("p (c t) -> p c t", t=512),
                            in_=dr[src].rearrange("(c p) t -> p c t", p=128)[
                                :, :, tp * 512:(tp + 1) * 512]), "LK")
                    k.mark(f"B5x{tp}", "LK")
                k.sec("sync", ldx)

                for mb in range(8):
                    def ldw(sync, tp=tp, mb=mb):
                        if mb >= 1:
                            k.need(sync, f"B5m1{tp}_{mb - 1}", "TE")
                        elif tp >= 1:
                            k.need(sync, f"B5m1{tp - 1}_7", "TE")
                        k.dinc(sync.dma_start(
                            out=wstg[:, :].rearrange("p (c f) -> p c f", f=512),
                            in_=dr["rw1"][:, mb * 512:(mb + 1) * 512]
                            .rearrange("(c p) f -> p c f", p=128)), "LW")
                        k.mark(f"B5w{tp}_{mb}", "LW")
                    k.sec("sync", ldw)

                    def spw(vector, tp=tp, mb=mb):
                        k.need(vector, f"B5w{tp}_{mb}", "LW")
                        if mb == 0 and tp >= 1:
                            k.need(vector, f"B5m2{tp - 1}_1_31", "TE")
                        vector.tensor_copy(w5H[:, :], wstg[:, :])
                        vector.tensor_tensor(w5L[:, :], wstg[:, :], w5H[:, :],
                                             ALU.subtract)
                        k.inc(vector.drain(), "VE")
                        k.mark(f"B5ws{tp}_{mb}", "VE")
                    k.sec("vector", spw)

                    def te1(tensor, tp=tp, mb=mb):
                        k.need(tensor, f"B5ws{tp}_{mb}", "VE")
                        if mb == 0:
                            k.need(tensor, f"B5x{tp}", "LK")
                        for mt in range(4):
                            ps = PS[2 + mt % 2]
                            if mt >= 2:
                                k.need(tensor, f"B5h{tp}_{mb}_{mt - 2}", "VE")
                            elif mb >= 1:
                                k.need(tensor, f"B5h{tp}_{mb - 1}_{2 + mt}",
                                       "VE")
                            elif tp >= 1:
                                k.need(tensor, f"B5h{tp - 1}_7_{2 + mt}", "VE")
                            for c in range(8):
                                lh = w5H[:, c * 512 + mt * 128:
                                         c * 512 + (mt + 1) * 128]
                                ll = w5L[:, c * 512 + mt * 128:
                                         c * 512 + (mt + 1) * 128]
                                rh = xslH[:, c * 512:(c + 1) * 512]
                                rl = xslL[:, c * 512:(c + 1) * 512]
                                tensor.matmul(ps[:, 0:512], lh, rh,
                                              start=(c == 0), stop=False)
                                tensor.matmul(ps[:, 0:512], lh, rl,
                                              start=False, stop=False)
                                tensor.matmul(ps[:, 0:512], ll, rh,
                                              start=False, stop=(c == 7))
                            k.inc(tensor.drain(), "TE")
                            k.mark(f"B5s{tp}_{mb}_{mt}", "TE")
                        k.mark(f"B5m1{tp}_{mb}", "TE")
                    k.sec("tensor", te1)

                    def ac1(scalar, tp=tp, mb=mb):
                        for mt in range(4):
                            u = mb * 4 + mt
                            k.need(scalar, f"B5s{tp}_{mb}_{mt}", "TE")
                            if u >= 2:
                                pm = divmod(u - 2, 4)
                                k.need(scalar, f"B5h{tp}_{pm[0]}_{pm[1]}", "VE")
                            scalar.activation(
                                out=evt[:, (u % 2) * 512:(u % 2 + 1) * 512],
                                in_=PS[2 + mt % 2][:, 0:512], func=AF.Gelu,
                                bias=rb1sb[:, u:u + 1], scale=1.0)
                            k.inc(scalar.drain(), "AC")
                            k.mark(f"B5g{tp}_{mb}_{mt}", "AC")
                    k.sec("scalar", ac1)

                    def ve1(vector, tp=tp, mb=mb):
                        for mt in range(4):
                            u = mb * 4 + mt
                            k.need(vector, f"B5g{tp}_{mb}_{mt}", "AC")
                            src = evt[:, (u % 2) * 512:(u % 2 + 1) * 512]
                            vector.tensor_copy(hfmH[:, u * 512:(u + 1) * 512],
                                               src)
                            vector.tensor_tensor(
                                hfmL[:, u * 512:(u + 1) * 512], src,
                                hfmH[:, u * 512:(u + 1) * 512], ALU.subtract)
                            k.inc(vector.drain(), "VE")
                            k.mark(f"B5h{tp}_{mb}_{mt}", "VE")
                    k.sec("vector", ve1)

                for nh in range(2):
                    for kc in range(32):
                        def ldw2(sync, tp=tp, nh=nh, kc=kc):
                            if kc >= 2:
                                k.need(sync, f"B5w2s{tp}_{nh}_{kc - 2}", "VE")
                            elif nh == 1:
                                k.need(sync, f"B5w2s{tp}_0_{30 + kc}", "VE")
                            elif tp >= 1:
                                k.need(sync, f"B5w2s{tp - 1}_1_{30 + kc}", "VE")
                            k.dinc(sync.dma_start(
                                out=ws2[:, (kc % 2) * 512:(kc % 2 + 1) * 512],
                                in_=dr["rw2"][kc * 128:(kc + 1) * 128,
                                              nh * 512:(nh + 1) * 512]), "LW")
                            k.mark(f"B5w2{tp}_{nh}_{kc}", "LW")
                        k.sec("sync", ldw2)

                        def spw2(vector, tp=tp, nh=nh, kc=kc):
                            k.need(vector, f"B5w2{tp}_{nh}_{kc}", "LW")
                            if kc >= 2:
                                k.need(vector, f"B5m2{tp}_{nh}_{kc - 2}", "TE")
                            elif nh == 1:
                                k.need(vector, f"B5m2{tp}_0_{30 + kc}", "TE")
                            elif tp >= 1:
                                k.need(vector, f"B5m2{tp - 1}_1_{30 + kc}", "TE")
                            sl = slice((kc % 2) * 512, (kc % 2 + 1) * 512)
                            vector.tensor_copy(w2h[:, sl], ws2[:, sl])
                            vector.tensor_tensor(w2l[:, sl], ws2[:, sl],
                                                 w2h[:, sl], ALU.subtract)
                            k.inc(vector.drain(), "VE")
                            k.mark(f"B5w2s{tp}_{nh}_{kc}", "VE")
                        k.sec("vector", spw2)

                        def te2(tensor, tp=tp, nh=nh, kc=kc):
                            k.need(tensor, f"B5w2s{tp}_{nh}_{kc}", "VE")
                            if kc == 0:
                                for mt2 in range(4):
                                    if nh == 0:
                                        k.need(tensor,
                                               f"B5h{tp}_7_{mt2}", "VE")
                                    else:
                                        k.need(tensor,
                                               f"B5e{tp}_0_{mt2}", "VE")
                            sl = slice((kc % 2) * 512, (kc % 2 + 1) * 512)
                            for mt2 in range(4):
                                ps = PS[2 + mt2]
                                lh = hfmH[:, kc * 512 + mt2 * 128:
                                          kc * 512 + (mt2 + 1) * 128]
                                ll = hfmL[:, kc * 512 + mt2 * 128:
                                          kc * 512 + (mt2 + 1) * 128]
                                tensor.matmul(ps[:, 0:512], lh, w2h[:, sl],
                                              start=(kc == 0), stop=False)
                                tensor.matmul(ps[:, 0:512], lh, w2l[:, sl],
                                              start=False, stop=False)
                                tensor.matmul(ps[:, 0:512], ll, w2h[:, sl],
                                              start=False, stop=(kc == 31))
                            k.inc(tensor.drain(), "TE")
                            k.mark(f"B5m2{tp}_{nh}_{kc}", "TE")
                        k.sec("tensor", te2)

                    def ve2(vector, tp=tp, nh=nh):
                        k.need(vector, f"B5m2{tp}_{nh}_31", "TE")
                        if nh == 0 and tp >= 1:
                            k.need(vector, f"B5rst{tp - 1}", "GSK")
                        for mt2 in range(4):
                            vector.tensor_add(
                                rms[:, mt2 * 1024 + nh * 512:
                                    mt2 * 1024 + (nh + 1) * 512],
                                PS[2 + mt2][:, 0:512],
                                rb2T[:, nh * 512:(nh + 1) * 512])
                            k.inc(vector.drain(), "VE")
                            k.mark(f"B5e{tp}_{nh}_{mt2}", "VE")
                    k.sec("vector", ve2)

                def ge(gpsimd, tp=tp):
                    for mt2 in range(4):
                        k.need(gpsimd, f"B5e{tp}_1_{mt2}", "VE")
                        k.dinc(gpsimd.dma_start(
                            out=dr["rmbuf"][tp * 512 + mt2 * 128:
                                            tp * 512 + (mt2 + 1) * 128, :],
                            in_=rms[:, mt2 * 1024:(mt2 + 1) * 1024]), "GSK")
                    k.mark(f"B5rst{tp}", "GSK")
                k.sec("gpsimd", ge)
        k.barrier("B5")

    def ph_B6():
        with (nc.sbuf_tensor("hef6H", [128, 32 * 384], BF16, allow_name_mangling=True) as hefmH,
              nc.sbuf_tensor("hef6L", [128, 32 * 384], BF16, allow_name_mangling=True) as hefmL,
              nc.sbuf_tensor("xef6H", [128, 8 * 640], BF16, allow_name_mangling=True) as xefH,
              nc.sbuf_tensor("xef6L", [128, 8 * 640], BF16, allow_name_mangling=True) as xefL,
              nc.sbuf_tensor("xestg6", [128, 1024], BF16, allow_name_mangling=True) as xestg,
              nc.sbuf_tensor("wstg6", [128, 4 * 512], F32, allow_name_mangling=True) as wstg,
              nc.sbuf_tensor("w6H", [128, 8 * 512], BF16, allow_name_mangling=True) as w6H,
              nc.sbuf_tensor("w6L", [128, 8 * 512], BF16, allow_name_mangling=True) as w6L,
              nc.sbuf_tensor("ws2b6", [128, 1024], F32, allow_name_mangling=True) as ws2b,
              nc.sbuf_tensor("w2hb6", [128, 2 * 1024], BF16, allow_name_mangling=True) as w2hb,
              nc.sbuf_tensor("w2lb6", [128, 2 * 1024], BF16, allow_name_mangling=True) as w2lb,
              nc.sbuf_tensor("yev6", [128, 2 * 1024], F32, allow_name_mangling=True) as yev):
            for e in range(E):
                def ldb(sync, e=e):
                    k.dinc(sync.dma_start(out=rowb[0:1, 0:1024],
                                          in_=dr["b2"][e:e + 1, :]), "LM2")
                    k.mark(f"B6br{e}", "LM2")
                k.sec("sync", ldb)

                def vsp(vector, e=e):
                    k.need(vector, f"B6br{e}", "LM2")
                    r = rowb[0:1, 0:1024]
                    vector.tensor_copy(rspH[0:1, :], r)
                    vector.tensor_tensor(rspL[0:1, :], r, rspH[0:1, :],
                                         ALU.subtract)
                    vector.tensor_tensor(evt[0:1, 0:1024], r, rspH[0:1, :],
                                         ALU.subtract)
                    vector.tensor_tensor(rsp2[0:1, :], evt[0:1, 0:1024],
                                         rspL[0:1, :], ALU.subtract)
                    k.inc(vector.drain(), "VE")
                    k.mark(f"B6bsp{e}", "VE")
                k.sec("vector", vsp)

                def tb(tensor, e=e):
                    k.need(tensor, f"B6bsp{e}", "VE")
                    for nh in range(2):
                        if nh == 1:
                            k.need(tensor, f"B6bcc{e}_0", "AC")
                        for j, src in enumerate((rspH, rspL, rsp2)):
                            tensor.matmul(PS[6][0:128, 0:512], sel0b[:, 0:128],
                                          src[:, nh * 512:(nh + 1) * 512],
                                          start=(j == 0), stop=(j == 2))
                        k.inc(tensor.drain(), "TE")
                        k.mark(f"B6bmm{e}_{nh}", "TE")
                k.sec("tensor", tb)

                def ab(scalar, e=e):
                    for nh in range(2):
                        k.need(scalar, f"B6bmm{e}_{nh}", "TE")
                        scalar.copy(b2e[:, nh * 512:(nh + 1) * 512],
                                    PS[6][0:128, 0:512])
                        k.inc(scalar.drain(), "AC")
                        k.mark(f"B6bcc{e}_{nh}", "AC")
                k.sec("scalar", ab)

                for st in range(5):
                    for lo in range(2):
                        def ldx(sync, e=e, st=st, lo=lo):
                            if st + lo > 0:
                                pst, plo = (st, 0) if lo == 1 else (st - 1, 1)
                                k.need(sync, f"B6xt{e}_{pst}_{plo}", "TE")
                            k.dinc(sync.dma_start(
                                out=xestg[:, :],
                                in_=dr["xeL" if lo else "xeH"][
                                    e * CAP + st * 128:e * CAP + (st + 1) * 128,
                                    :]), "LX")
                            k.mark(f"B6xr{e}_{st}_{lo}", "LX")
                        k.sec("sync", ldx)

                        def tex(tensor, e=e, st=st, lo=lo):
                            k.need(tensor, f"B6xr{e}_{st}_{lo}", "LX")
                            if st + lo > 0:
                                pst, plo = (st, 0) if lo == 1 else (st - 1, 1)
                                k.need(tensor, f"B6xc{e}_{pst}_{plo}", "VE")
                            for c in range(8):
                                tensor.transpose(
                                    PSB[0:128, c * 128:(c + 1) * 128],
                                    xestg[:, c * 128:(c + 1) * 128], idb[:, :])
                            k.inc(tensor.drain(), "TE")
                            k.mark(f"B6xt{e}_{st}_{lo}", "TE")
                        k.sec("tensor", tex)

                        def vex(vector, e=e, st=st, lo=lo):
                            k.need(vector, f"B6xt{e}_{st}_{lo}", "TE")
                            dst = (xefL if lo else xefH).rearrange(
                                "p (c s) -> p c s", s=640)[
                                :, :, st * 128:(st + 1) * 128]
                            vector.tensor_copy(
                                dst, PSB[0:128, :].rearrange(
                                    "p (c q) -> p c q", q=128))
                            k.inc(vector.drain(), "VE")
                            k.mark(f"B6xc{e}_{st}_{lo}", "VE")
                        k.sec("vector", vex)

                for sw, st2s in ((0, (0, 1, 2)), (1, (3, 4))):
                    NS = len(st2s) * 128
                    s0 = st2s[0] * 128
                    for mb in range(8):
                        for ch in range(2):
                            def ldw(sync, e=e, sw=sw, mb=mb, ch=ch):
                                if ch == 1:
                                    k.need(sync, f"B6w1s{e}_{sw}_{mb}_0", "VE")
                                elif mb >= 1:
                                    k.need(sync, f"B6w1s{e}_{sw}_{mb - 1}_1",
                                           "VE")
                                elif sw == 1:
                                    k.need(sync, f"B6w1s{e}_0_7_1", "VE")
                                k.dinc(sync.dma_start(
                                    out=wstg[:, :].rearrange(
                                        "p (c f) -> p c f", f=512),
                                    in_=dr["w1"][e * 1024 + ch * 512:
                                                 e * 1024 + (ch + 1) * 512,
                                                 mb * 512:(mb + 1) * 512]
                                    .rearrange("(c p) f -> p c f", p=128)),
                                    "LW")
                                k.mark(f"B6w1{e}_{sw}_{mb}_{ch}", "LW")
                            k.sec("sync", ldw)

                            def spw(vector, e=e, sw=sw, mb=mb, ch=ch):
                                k.need(vector, f"B6w1{e}_{sw}_{mb}_{ch}", "LW")
                                if ch == 0:
                                    if mb >= 1:
                                        k.need(vector, f"B6s{e}_{sw}_{mb - 1}_3",
                                               "TE")
                                    elif sw == 1:
                                        k.need(vector, f"B6s{e}_0_7_3", "TE")
                                dh = w6H[:, ch * 4 * 512:(ch + 1) * 4 * 512]\
                                    .rearrange("p (c f) -> p c f", f=512)
                                dl = w6L[:, ch * 4 * 512:(ch + 1) * 4 * 512]\
                                    .rearrange("p (c f) -> p c f", f=512)
                                wv = wstg[:, :].rearrange("p (c f) -> p c f",
                                                          f=512)
                                vector.tensor_copy(dh, wv)
                                vector.tensor_tensor(dl, wv, dh, ALU.subtract)
                                k.inc(vector.drain(), "VE")
                                k.mark(f"B6w1s{e}_{sw}_{mb}_{ch}", "VE")
                            k.sec("vector", spw)

                        def te1(tensor, e=e, sw=sw, mb=mb, NS=NS, s0=s0):
                            k.need(tensor, f"B6w1s{e}_{sw}_{mb}_1", "VE")
                            if mb == 0:
                                k.need(tensor, f"B6xc{e}_4_1", "VE")
                            for mt in range(4):
                                ps = PS[2 + mt % 2]
                                if mt >= 2:
                                    k.need(tensor, f"B6h{e}_{sw}_{mb}_{mt - 2}",
                                           "VE")
                                elif mb >= 1:
                                    k.need(tensor,
                                           f"B6h{e}_{sw}_{mb - 1}_{2 + mt}",
                                           "VE")
                                elif sw == 1:
                                    k.need(tensor, f"B6ye{e}_1", "VE")
                                for c in range(8):
                                    lh = w6H[:, c * 512 + mt * 128:
                                             c * 512 + (mt + 1) * 128]
                                    ll = w6L[:, c * 512 + mt * 128:
                                             c * 512 + (mt + 1) * 128]
                                    rh = xefH[:, c * 640 + s0:c * 640 + s0 + NS]
                                    rl = xefL[:, c * 640 + s0:c * 640 + s0 + NS]
                                    tensor.matmul(ps[:, 0:NS], lh, rh,
                                                  start=(c == 0), stop=False)
                                    tensor.matmul(ps[:, 0:NS], lh, rl,
                                                  start=False, stop=False)
                                    tensor.matmul(ps[:, 0:NS], ll, rh,
                                                  start=False, stop=(c == 7))
                                k.inc(tensor.drain(), "TE")
                                k.mark(f"B6s{e}_{sw}_{mb}_{mt}", "TE")
                        k.sec("tensor", te1)

                        def ac1(scalar, e=e, sw=sw, mb=mb, NS=NS):
                            for mt in range(4):
                                u = mb * 4 + mt
                                k.need(scalar, f"B6s{e}_{sw}_{mb}_{mt}", "TE")
                                if u >= 2:
                                    pm = divmod(u - 2, 4)
                                    k.need(scalar,
                                           f"B6h{e}_{sw}_{pm[0]}_{pm[1]}",
                                           "VE")
                                buf = evt if u % 2 == 0 else evt2
                                scalar.activation(
                                    out=buf[:, 0:NS],
                                    in_=PS[2 + mt % 2][:, 0:NS], func=AF.Gelu,
                                    bias=b1sb[:, e * 32 + u:e * 32 + u + 1],
                                    scale=1.0)
                                k.inc(scalar.drain(), "AC")
                                k.mark(f"B6g{e}_{sw}_{mb}_{mt}", "AC")
                        k.sec("scalar", ac1)

                        def ve1(vector, e=e, sw=sw, mb=mb, NS=NS):
                            for mt in range(4):
                                u = mb * 4 + mt
                                k.need(vector, f"B6g{e}_{sw}_{mb}_{mt}", "AC")
                                if sw == 1 and u == 0:
                                    k.need(vector, f"B6m2{e}_0_31", "TE")
                                buf = evt if u % 2 == 0 else evt2
                                src = buf[:, 0:NS]
                                vector.tensor_copy(
                                    hefmH[:, u * 384:u * 384 + NS], src)
                                vector.tensor_tensor(
                                    hefmL[:, u * 384:u * 384 + NS], src,
                                    hefmH[:, u * 384:u * 384 + NS],
                                    ALU.subtract)
                                k.inc(vector.drain(), "VE")
                                k.mark(f"B6h{e}_{sw}_{mb}_{mt}", "VE")
                        k.sec("vector", ve1)

                    for kc in range(32):
                        def ldw2(sync, e=e, sw=sw, kc=kc):
                            if kc >= 1:
                                k.need(sync, f"B6w2s{e}_{sw}_{kc - 1}", "VE")
                            elif sw == 1:
                                k.need(sync, f"B6w2s{e}_0_31", "VE")
                            k.dinc(sync.dma_start(
                                out=ws2b[:, :],
                                in_=dr["w2"][e * FF + kc * 128:
                                             e * FF + (kc + 1) * 128, :]),
                                "LW")
                            k.mark(f"B6w2{e}_{sw}_{kc}", "LW")
                        k.sec("sync", ldw2)

                        def spw2(vector, e=e, sw=sw, kc=kc):
                            k.need(vector, f"B6w2{e}_{sw}_{kc}", "LW")
                            if kc >= 2:
                                k.need(vector, f"B6m2{e}_{sw}_{kc - 2}", "TE")
                            elif sw == 1:
                                k.need(vector, f"B6m2{e}_0_{30 + kc}", "TE")
                            sl = slice((kc % 2) * 1024, (kc % 2 + 1) * 1024)
                            vector.tensor_copy(w2hb[:, sl], ws2b[:, :])
                            vector.tensor_tensor(w2lb[:, sl], ws2b[:, :],
                                                 w2hb[:, sl], ALU.subtract)
                            k.inc(vector.drain(), "VE")
                            k.mark(f"B6w2s{e}_{sw}_{kc}", "VE")
                        k.sec("vector", spw2)

                        def te2(tensor, e=e, sw=sw, kc=kc, st2s=st2s):
                            k.need(tensor, f"B6w2s{e}_{sw}_{kc}", "VE")
                            if kc == 0:
                                k.need(tensor, f"B6g{e}_{sw}_7_2", "AC")
                                k.need(tensor, f"B6g{e}_{sw}_7_3", "AC")
                                if sw == 0:
                                    k.need(tensor, f"B6xc{e}_4_1", "VE")
                                else:
                                    for st2 in (0, 1, 2):
                                        k.need(tensor, f"B6ye{e}_{st2}", "VE")
                            for st2 in st2s:
                                for nh in range(2):
                                    ps = PS[(st2 - st2s[0]) * 2 + nh]
                                    i0 = kc * 384 + (st2 - st2s[0]) * 128
                                    lh = hefmH[:, i0:i0 + 128]
                                    ll = hefmL[:, i0:i0 + 128]
                                    rh = w2hb[:, (kc % 2) * 1024 + nh * 512:
                                              (kc % 2) * 1024 + (nh + 1) * 512]
                                    rl = w2lb[:, (kc % 2) * 1024 + nh * 512:
                                              (kc % 2) * 1024 + (nh + 1) * 512]
                                    tensor.matmul(ps[:, 0:512], lh, rh,
                                                  start=(kc == 0), stop=False)
                                    tensor.matmul(ps[:, 0:512], lh, rl,
                                                  start=False, stop=False)
                                    tensor.matmul(ps[:, 0:512], ll, rh,
                                                  start=False, stop=(kc == 31))
                            k.inc(tensor.drain(), "TE")
                            k.mark(f"B6m2{e}_{sw}_{kc}", "TE")
                        k.sec("tensor", te2)

                    def ve2(vector, e=e, sw=sw, st2s=st2s):
                        k.need(vector, f"B6m2{e}_{sw}_31", "TE")
                        for st2 in st2s:
                            if st2 >= 2:
                                k.need(vector, f"B6yst{e}_{st2 - 2}", "GSV")
                            k.need(vector, f"B6bcc{e}_1", "AC")
                            for nh in range(2):
                                ps = PS[(st2 - st2s[0]) * 2 + nh]
                                vector.tensor_add(
                                    yev[:, (st2 % 2) * 1024 + nh * 512:
                                        (st2 % 2) * 1024 + (nh + 1) * 512],
                                    ps[:, 0:512],
                                    b2e[:, nh * 512:(nh + 1) * 512])
                            k.inc(vector.drain(), "VE")
                            k.mark(f"B6ye{e}_{st2}", "VE")
                    k.sec("vector", ve2)

                    def gey(gpsimd, e=e, st2s=st2s):
                        for st2 in st2s:
                            k.need(gpsimd, f"B6ye{e}_{st2}", "VE")
                            k.dinc(gpsimd.dma_start(
                                out=dr["ybuf"][e * CAP + st2 * 128:
                                               e * CAP + (st2 + 1) * 128, :],
                                in_=yev[:, (st2 % 2) * 1024:
                                        (st2 % 2 + 1) * 1024]), "GSV")
                            k.mark(f"B6yst{e}_{st2}", "GSV")
                    k.sec("gpsimd", gey)
                k.barrier(f"B6x{e}")
        k.barrier("B6")

    def ph_B7():
        with (nc.sbuf_tensor("yg7", [128, 4 * 1024], F32, allow_name_mangling=True) as yg,
              nc.sbuf_tensor("xi7", [128, 4 * 1024], F32, allow_name_mangling=True) as xi7,
              nc.sbuf_tensor("rm7", [128, 4 * 1024], F32, allow_name_mangling=True) as rm7,
              nc.sbuf_tensor("x17", [128, 4 * 1024], F32, allow_name_mangling=True) as x17,
              nc.sbuf_tensor("inn7", [128, 4 * 1024], F32, allow_name_mangling=True) as inn,
              nc.sbuf_tensor("ot7", [128, 4 * 1024], F32, allow_name_mangling=True) as ot,
              nc.sbuf_tensor("b7c", [128, 4], F32, allow_name_mangling=True) as b7c):
            for ts in range(4):
                def ge1(gpsimd, ts=ts):
                    if ts >= 1:
                        k.need(gpsimd, f"B7mx{ts - 1}", "VE")
                    for tt in range(4):
                        t = ts * 4 + tt
                        k.dinc(gpsimd.indirect_dma_start(
                            out=yg[:, tt * 1024:(tt + 1) * 1024],
                            out_offset=None, in_=dr["ybuf"],
                            in_offset=IndirectOffsetOnAxis(
                                dest32[:, t:t + 1], axis=0)), "GI")
                    k.mark(f"B7y{ts}", "GI")
                k.sec("gpsimd", ge1)

                def ld(sync, ts=ts):
                    if ts >= 1:
                        k.need(sync, f"B7mx{ts - 1}", "VE")
                        k.need(sync, f"B7o{ts - 1}", "VE")
                    for tt in range(4):
                        r0 = ts * 512 + tt * 128
                        k.dinc(sync.dma_start(
                            out=xi7[:, tt * 1024:(tt + 1) * 1024],
                            in_=dr["xibuf"][r0:r0 + 128, :]), "LX")
                        k.dinc(sync.dma_start(
                            out=rm7[:, tt * 1024:(tt + 1) * 1024],
                            in_=dr["rmbuf"][r0:r0 + 128, :]), "LK")
                        k.dinc(sync.dma_start(
                            out=x17[:, tt * 1024:(tt + 1) * 1024],
                            in_=dr["x1buf"][r0:r0 + 128, :]), "LW")
                    k.mark(f"B7x{ts}", "LX")
                    k.mark(f"B7r{ts}", "LK")
                    k.mark(f"B7f{ts}", "LW")
                k.sec("sync", ld)

                def ve1(vector, ts=ts):
                    k.need(vector, f"B7y{ts}", "GI")
                    k.need(vector, f"B7x{ts}", "LX")
                    k.need(vector, f"B7r{ts}", "LK")
                    for tt in range(4):
                        t = ts * 4 + tt
                        sl = slice(tt * 1024, (tt + 1) * 1024)
                        vector.tensor_mul(b7c[:, tt:tt + 1], gvt[:, t:t + 1],
                                          c0t[:, t:t + 1])
                        vector.drain()
                        vector.tensor_scalar(yg[:, sl], yg[:, sl],
                                             b7c[:, tt:tt + 1], None, ALU.mult)
                        vector.drain()
                        vector.scalar_tensor_tensor(
                            inn[:, sl], rm7[:, sl], c1t[:, t:t + 1],
                            yg[:, sl], ALU.mult, ALU.add)
                        vector.drain()
                        vector.tensor_tensor(inn[:, sl], inn[:, sl],
                                             xi7[:, sl], ALU.add)
                    k.inc(vector.drain(), "VE")
                    k.mark(f"B7mx{ts}", "VE")
                k.sec("vector", ve1)

                def acg(scalar, ts=ts):
                    k.need(scalar, f"B7mx{ts}", "VE")
                    k.inc(scalar.drain(), "AC")
                k.sec("scalar", acg)
                ln_stats(f"B7l{ts}",
                         lambda i: inn[:, i * 1024:(i + 1) * 1024], 4)

                def ve2(vector, ts=ts):
                    k.need(vector, f"B7f{ts}", "LW")
                    if ts >= 1:
                        k.need(vector, f"B7st{ts - 1}", "GO")
                    for tt in range(4):
                        sl = slice(tt * 1024, (tt + 1) * 1024)
                        vector.tensor_scalar(ot[:, sl], inn[:, sl],
                                             stats[:, 12 + tt:13 + tt],
                                             stats[:, 20 + tt:21 + tt],
                                             ALU.mult, ALU.add)
                    vector.drain()
                    for tt in range(4):
                        sl = slice(tt * 1024, (tt + 1) * 1024)
                        vector.tensor_tensor(ot[:, sl], ot[:, sl],
                                             x17[:, sl], ALU.add)
                    k.inc(vector.drain(), "VE")
                    k.mark(f"B7o{ts}", "VE")
                k.sec("vector", ve2)

                def ge2(gpsimd, ts=ts):
                    k.need(gpsimd, f"B7o{ts}", "VE")
                    for tt in range(4):
                        r0 = ts * 512 + tt * 128
                        k.dinc(gpsimd.dma_start(
                            out=dr["out"][r0:r0 + 128, :],
                            in_=ot[:, tt * 1024:(tt + 1) * 1024]), "GO")
                    k.mark(f"B7st{ts}", "GO")
                k.sec("gpsimd", ge2)
        k.barrier("B7")

    with nc.Block() as blk:
        k.blk = blk
        k.run2(ph_setup)
        if STOP_AFTER == "SU":
            _close(k)
            return k
        if A1_MODE == "skipA1":
            pass
        else:
            k.run2(ph_A1)
        boT = A("boT", [128, D], F32); rb2T = A("rb2T", [128, D], F32)
        g1T = A("g1T", [128, D], F32); b1T = A("b1T", [128, D], F32)
        g2T = A("g2T", [128, D], F32); b2T = A("b2T", [128, D], F32)
        vbqT = A("vbqT", [128, D], F32)
        b2e = A("b2e", [128, D], F32)
        evt2 = A("evt2", [128, 1024], F32)
        ph_setup2_refs.update(boT=boT, rb2T=rb2T, g1T=g1T, b1T=b1T, g2T=g2T,
                              b2T=b2T, vbqT=vbqT, b2e=b2e, evt2=evt2)
        if A1_MODE != "nosu2":
            k.run2(ph_setup2)
        if STOP_AFTER == "A1":
            _close(k)
            return k
        k.run2(lambda: ph_bounds("A"))
        k.run2(ph_A2)
        if STOP_AFTER == "A2":
            _close(k)
            return k
        k.run2(ph_A3)
        if STOP_AFTER == "A3":
            _close(k)
            return k
        k.run2(ph_B1)
        k.run2(lambda: ph_bounds("B"))
        if STOP_AFTER == "B1":
            _close(k)
            return k
        k.run2(ph_B2)
        if STOP_AFTER == "B2":
            _close(k)
            return k
        k.run2(ph_B3a)
        k.run2(ph_B3b)
        if STOP_AFTER == "B3":
            _close(k)
            return k
        k.run2(ph_B4)
        if STOP_AFTER == "B4":
            _close(k)
            return k
        k.run2(ph_B5)
        if STOP_AFTER == "B5":
            _close(k)
            return k
        k.run2(ph_B6)
        if STOP_AFTER == "B6":
            _close(k)
            return k
        k.run2(ph_B7)
        _close(k)
    return k


def _close(k):
    for ds in k.DMASEMS:
        k.mark(f"fin_{ds}", ds)
    k.emitting = True

    def ge(gpsimd):
        for ds in k.DMASEMS:
            k.need(gpsimd, f"fin_{ds}", ds)
    k.cur_eng = "gpsimd"
    k.blk.gpsimd(ge)
    k.cur_eng = None
    k.emitting = False



def _consts():
    c = {}
    c["identf"] = np.eye(128, dtype=np.float32)
    c["identb"] = np.eye(128, dtype=ml_dtypes.bfloat16)
    p = np.arange(128)[:, None]
    f = np.arange(128)[None, :]
    mlo = np.where(p <= f, 0.0, NEG).astype(np.float32)
    mhi = np.where(p > f, 0.0, NEG).astype(np.float32)
    c["masklo4"] = np.tile(mlo, (1, 4)).copy()
    c["maskhi4"] = np.tile(mhi, (1, 4)).copy()
    hs = np.zeros((8, 128, 16), np.float32)
    for ch in range(8):
        for pp in range(128):
            hs[ch, pp, 2 * ch + pp // 64] = 1.0
    c["hsumb"] = hs.reshape(8 * 128, 16).astype(ml_dtypes.bfloat16)
    c["trilb"] = np.tril(np.ones((128, 128), np.float32), -1).T.astype(
        ml_dtypes.bfloat16)
    c["onesb128"] = np.ones((128, 128), ml_dtypes.bfloat16)
    c["i16f"] = np.eye(16, dtype=np.float32)
    c["i16b"] = np.eye(16, dtype=ml_dtypes.bfloat16)
    sel0 = np.zeros((128, 128), np.float32); sel0[0, :] = 1.0
    c["sel0b"] = sel0.astype(ml_dtypes.bfloat16)
    sel64 = np.zeros((128, 64), np.float32); sel64[64, :] = 1.0
    c["sel64b"] = sel64.astype(ml_dtypes.bfloat16)
    return c


def kernel(L0_attn_x, L0_attn_Wq, L0_attn_Wk, L0_attn_Wv, L0_attn_Wo,
           L0_ffn_ln1_g, L0_ffn_ln1_b, L0_ffn_Wqkv, L0_ffn_bqkv, L0_ffn_Wo,
           L0_ffn_bo, L0_ffn_ln2_g, L0_ffn_ln2_b, L0_ffn_gate_w, L0_ffn_W1,
           L0_ffn_b1, L0_ffn_W2, L0_ffn_b2, L0_ffn_rW1, L0_ffn_rb1,
           L0_ffn_rW2, L0_ffn_rb2, L0_ffn_coef_w, L0_ffn_coef_b):
    f32 = lambda a: np.ascontiguousarray(np.asarray(a, dtype=np.float32))
    x = f32(L0_attn_x)
    B = x.shape[0]
    shared = {
        "wqT": f32(np.asarray(L0_attn_Wq).T),
        "wkT": f32(np.asarray(L0_attn_Wk).T),
        "wvT": f32(np.asarray(L0_attn_Wv).T),
        "woT": f32(np.asarray(L0_attn_Wo).T),
        "wqkv": f32(L0_ffn_Wqkv), "bqkv": f32(L0_ffn_bqkv),
        "fwo": f32(L0_ffn_Wo), "bo": f32(L0_ffn_bo),
        "ln1g": f32(L0_ffn_ln1_g), "ln1b": f32(L0_ffn_ln1_b),
        "ln2g": f32(L0_ffn_ln2_g), "ln2b": f32(L0_ffn_ln2_b),
        "gatew": f32(L0_ffn_gate_w),
        "w1": f32(np.asarray(L0_ffn_W1).reshape(E * D, FF)),
        "b1": f32(L0_ffn_b1),
        "w2": f32(np.asarray(L0_ffn_W2).reshape(E * FF, D)),
        "b2": f32(L0_ffn_b2),
        "rw1": f32(L0_ffn_rW1), "rb1": f32(L0_ffn_rb1),
        "rw2": f32(L0_ffn_rW2), "rb2": f32(L0_ffn_rb2),
        "coefw": np.ascontiguousarray(np.pad(
            f32(L0_ffn_coef_w), ((0, 0), (0, 6)))),
        "coefb": f32(L0_ffn_coef_b),
    }
    shared.update(_consts())

    nc = bass.Bass("TRN2", target_bir_lowering=False)
    build(nc)

    in_maps = []
    for c in range(8):
        b, h = c // 2, c % 2
        m = dict(shared)
        m["x"] = np.ascontiguousarray(
            np.concatenate([x[b, h * T2:(h + 1) * T2],
                            x[b, (1 - h) * T2:(2 - h) * T2]], axis=0))
        m["haloAin"] = np.full((128, 1), 0.0 if h == 1 else NEG, np.float32)
        m["haloBin"] = np.full((128, 1), 0.0 if h == 0 else NEG, np.float32)
        in_maps.append(m)

    res = run_bass_kernel_spmd(nc, in_maps, core_ids=list(range(8)))
    out = np.empty((B, T4, D), np.float32)
    for c in range(8):
        b, h = c // 2, c % 2
        out[b, h * T2:(h + 1) * T2] = res.results[c]["out"]
    return out
